# version 05
# baseline (speedup 1.0000x reference)
import numpy as np
import ml_dtypes
import contextlib
import concourse.bass as bass
import concourse.mybir as mybir
from concourse.bass_utils import run_bass_kernel_spmd

F32 = mybir.dt.float32
BF16 = mybir.dt.bfloat16
AF = mybir.ActivationFunctionType
ALU = mybir.AluOpType
AX = mybir.AxisListType

P = 128
T = 2048
D = 1024
NT = 16
E = 8
KC = 8
H = 8
DH = 128
DI = 256
NS = 16
DTR = 8
C = 256
NCH = T // C
G = 16

ENGINES = ("sync", "gpsimd", "vector", "scalar", "tensor")


class Sched:

    def __init__(self, sems, paranoid=False):
        self.ops = {e: [] for e in ENGINES}
        self.sems = sems
        self.v = {n: 0 for n in sems}
        self.paranoid = paranoid

    SELF = {"vector": "s_dve", "scalar": "s_act"}

    def emit(self, eng, fn, waits=(), inc=None, inc_n=1, drain=False):
        if self.paranoid:
            waits = [(n, v) for n, v in self.v.items()]
        else:
            own = self.SELF.get(eng)
            if own is not None:
                waits = list(waits) + [(own, self.v[own])]
        w = [(self.sems[s], v) for (s, v) in waits if v > 0]
        self.ops[eng].append((w, fn, (self.sems[inc], inc_n) if inc else None,
                              drain))
        if inc:
            self.v[inc] += inc_n

    def barrier_vals(self):
        return dict(self.v)

    def wait_all(self):
        return [(n, self.v[n]) for n in self.sems]

    def finalize(self, block):
        for eng in ENGINES:
            ops = self.ops[eng]
            if not ops:
                continue

            def make(eng_ops):
                def body(e):
                    seen = {}
                    for waits, fn, inc, drain in eng_ops:
                        for s, v in waits:
                            if seen.get(s.name, -1) < v:
                                e.wait_ge(s, v)
                                seen[s.name] = v
                        if drain:
                            e.drain()
                        ins = fn(e)
                        if inc is not None:
                            ins.then_inc(inc[0], inc[1])
                return body
            getattr(block, eng)(make(ops))


def build(debug_tap=None, debug_scr=False, upto="full", paranoid=False):
    nc = bass.Bass("TRN2", target_bir_lowering=False)
    scr_kind = "ExternalOutput" if debug_scr else "Internal"

    x_d = nc.dram_tensor("x", [T, D], F32, kind="ExternalInput").ap()
    W_in_d = nc.dram_tensor("W_in", [H, DH, 2 * DI], F32, kind="ExternalInput").ap()
    conv_w_d = nc.dram_tensor("conv_w", [H, DI, 4], F32, kind="ExternalInput").ap()
    conv_b_d = nc.dram_tensor("conv_b", [H, DI], F32, kind="ExternalInput").ap()
    W_xp_d = nc.dram_tensor("W_xp", [H, DI, DTR + 2 * NS], F32, kind="ExternalInput").ap()
    W_dt_d = nc.dram_tensor("W_dt", [H, DTR, DI], F32, kind="ExternalInput").ap()
    b_dt_d = nc.dram_tensor("b_dt", [H, DI], F32, kind="ExternalInput").ap()
    A_log_d = nc.dram_tensor("A_log", [H, DI, NS], F32, kind="ExternalInput").ap()
    Dp_d = nc.dram_tensor("Dp", [H, DI], F32, kind="ExternalInput").ap()
    W_out_d = nc.dram_tensor("W_out", [H, DI, DH], F32, kind="ExternalInput").ap()
    W_ex_d = nc.dram_tensor("W_ex", [E, D, D], F32, kind="ExternalInput").ap()
    b_ex_d = nc.dram_tensor("b_ex", [E, D], F32, kind="ExternalInput").ap()
    W_rt_d = nc.dram_tensor("W_rt", [E, D], F32, kind="ExternalInput").ap()
    b_rt_d = nc.dram_tensor("b_rt", [1, E], F32, kind="ExternalInput").ap()
    idm_d = nc.dram_tensor("idm", [P, P], BF16, kind="ExternalInput").ap()

    out_d = nc.dram_tensor("out", [T, D], F32, kind="ExternalOutput").ap()
    tap_d = None
    if debug_tap is not None:
        tap_d = nc.dram_tensor("tap", list(debug_tap), F32, kind="ExternalOutput").ap()

    x_scr = nc.dram_tensor("x_scr", [NT, P, D], F32, kind=scr_kind).ap()
    xT2_scr = nc.dram_tensor("xT2_scr", [KC, P, T], BF16, kind="Internal").ap()
    wT_scr = nc.dram_tensor("wT_scr", [E, P, KC * D], BF16, kind="Internal").ap()
    bc_scr = nc.dram_tensor("bc_scr", [H, 2, NS, T], F32, kind=scr_kind).ap()
    dt_scr = nc.dram_tensor("dt_scr", [H, DI, T], F32, kind=scr_kind).ap()
    u_scr = nc.dram_tensor("u_scr", [H, DI, T], F32, kind=scr_kind).ap()
    y_scr = nc.dram_tensor("y_scr", [H, DI, T], F32, kind=scr_kind).ap()

    with contextlib.ExitStack() as st:
        def sb(name, pdim, cols_f32):
            return st.enter_context(nc.sbuf_tensor(name, [pdim, cols_f32], F32))

        AR1 = sb("AR1", P, 20480)
        AR2 = sb("AR2", P, 16384)
        AR3 = sb("AR3", P, 8192)

        W_in_bf = st.enter_context(nc.sbuf_tensor("W_in_bf", [P, H * 512], BF16))
        W_xp_bf = st.enter_context(nc.sbuf_tensor("W_xp_bf", [P, H * 2 * 40], BF16))
        W_dt_bf = st.enter_context(nc.sbuf_tensor("W_dt_bf", [DTR, H * DI], BF16))
        W_out_bf = st.enter_context(nc.sbuf_tensor("W_out_bf", [P, H * 2 * DH], BF16))
        W_rtT_bf = st.enter_context(nc.sbuf_tensor("W_rtT_bf", [P, KC * E], BF16))
        b_rt_bf = st.enter_context(nc.sbuf_tensor("b_rt_bf", [1, E], BF16))
        idm = st.enter_context(nc.sbuf_tensor("idm_s", [P, P], BF16))
        ones_bf = st.enter_context(nc.sbuf_tensor("ones_bf", [1, P], BF16))
        A_neg = sb("A_neg", P, G * NS)
        conv_w_s = sb("conv_w_s", P, G * 4)
        conv_b_s = sb("conv_b_s", P, G)
        Dp_s = sb("Dp_s", P, G)
        b_dt_s = sb("b_dt_s", P, G)
        w_all = sb("w_all", P, NT * E)
        carry = sb("carry", P, G * NS)
        small = sb("small", P, 64)
        dtr_bf = st.enter_context(nc.sbuf_tensor("dtr_bf", [DTR, C], BF16))
        bcs = sb("bcs", P, 2 * C)

        PS = [st.enter_context(nc.psum_tensor(f"PS{i}", [P, 512], F32))
              for i in range(6)]
        PBF = st.enter_context(nc.psum_tensor("PBF", [P, 1024], BF16))

        semnames = ["s_ld", "s_gd", "s_pe", "s_act", "s_dve"]
        sems = {n: st.enter_context(nc.semaphore(n)) for n in semnames}
        block = st.enter_context(nc.Block())
        S = Sched(sems, paranoid=(paranoid is True))
        def set_para(phase):
            if isinstance(paranoid, (set, frozenset, list, tuple)):
                S.paranoid = phase in paranoid

        def v32(ar, off, cols, pdim=P):
            return ar[0:pdim, off:off + cols]

        def vbf(ar, off_f32, cols_bf, pdim=P):
            return ar[0:pdim].bitcast(BF16)[0:pdim, 2 * off_f32:2 * off_f32 + cols_bf]

        b_ex_bf = vbf(AR1, 13888, D, pdim=E)
        wT8_bf = vbf(AR1, 14400, T, pdim=E)

        def xtile(i):
            return v32(AR2, i * D, D)

        def WTv(slot):
            return vbf(AR1, slot * 4096, 8192)
        WSTG = [v32(AR1, 8192 + i * 1024, 1024) for i in range(2)]
        WSTG_BF = [vbf(AR1, 10240 + i * 512, 1024) for i in range(2)]
        XNORM = [vbf(AR1, 11264 + i * 512, 1024) for i in range(2)]
        PACK = vbf(AR1, 12288, 1024)
        SQ = v32(AR1, 12800, 1024)
        RL = v32(AR1, 13824, 8)
        RW = [v32(AR1, 13832 + i * 8, 8) for i in range(4)]

        XT = AR3[:].bitcast(BF16)

        DA = [v32(AR1, i * 4096, 4096) for i in range(2)]
        DBX = [v32(AR1, 8192, 4096)] * 2
        BREP = v32(AR1, 12288, 4096)
        CREP = v32(AR1, 16384, 4096)
        def xinv(g2, cols=C + 4, off=0):
            return v32(AR2, g2 * (C + 4) + off, cols)
        SZOFF = G * (C + 4)
        def szv(g2):
            return v32(AR2, SZOFF + g2 * C, C)
        DTOFF = SZOFF + G * C
        def dtv(g2):
            return v32(AR2, DTOFF + g2 * C, C)
        yblk = dtv
        XRES = [v32(AR2, 12352 + i * D, D) for i in range(2)]
        SPX = [v32(AR2, 14400 + i * C, C) for i in range(3)]
        CONVACC = [v32(AR2, 15168 + i * C, C) for i in range(2)]
        UBLK = [v32(AR2, 15680 + i * C, C) for i in range(2)]
        def xcbfv(g2):
            return vbf(AR3, g2 * 128, 256)
        def ybfv(g2):
            return vbf(AR3, 2048 + g2 * 128, 256)
        DTI = [v32(AR3, 4096 + i * C, C) for i in range(2)]
        UI = [v32(AR3, 4608 + i * C, C) for i in range(2)]
        YI = [v32(AR3, 5120 + i * C, C) for i in range(2)]
        XT2C = [vbf(AR3, 5632 + i * 1024, 2048) for i in range(2)]

        LD, GD, PE, ACT, DVE = "s_ld", "s_gd", "s_pe", "s_act", "s_dve"

        set_para("p0")
        for i in range(NT):
            S.emit("sync", (lambda i=i: lambda e: e.dma_start(
                out=xtile(i), in_=x_d[i * P:(i + 1) * P, :]))(), inc=LD, inc_n=16)
        S.emit("sync", lambda e: e.dma_start(out=idm[:], in_=idm_d), inc=LD, inc_n=16)
        for h in range(H):
            S.emit("sync", (lambda h=h: lambda e: e.dma_start(
                out=v32(AR1, 0, 512), in_=W_in_d[h]))(), inc=LD, inc_n=16)
            S.emit("vector", (lambda h=h: lambda e: e.tensor_copy(
                W_in_bf[:, h * 512:(h + 1) * 512], v32(AR1, 0, 512)))(),
                waits=[(LD, S.v[LD])], inc=DVE)
            for k in range(2):
                S.emit("sync", (lambda h=h, k=k: lambda e: e.dma_start(
                    out=v32(AR1, 1024, 40), in_=W_xp_d[h, k * 128:(k + 1) * 128, :]))(),
                    waits=[(DVE, S.v[DVE])], inc=LD, inc_n=16)
                S.emit("vector", (lambda h=h, k=k: lambda e: e.tensor_copy(
                    W_xp_bf[:, (h * 2 + k) * 40:(h * 2 + k + 1) * 40],
                    v32(AR1, 1024, 40)))(), waits=[(LD, S.v[LD])], inc=DVE)
            S.emit("sync", (lambda h=h: lambda e: e.dma_start(
                out=v32(AR1, 2048, DI, pdim=DTR), in_=W_dt_d[h]))(),
                waits=[(DVE, S.v[DVE])], inc=LD, inc_n=16)
            S.emit("vector", (lambda h=h: lambda e: e.tensor_copy(
                W_dt_bf[:, h * DI:(h + 1) * DI], v32(AR1, 2048, DI, pdim=DTR)))(),
                waits=[(LD, S.v[LD])], inc=DVE)
            for k in range(2):
                S.emit("sync", (lambda h=h, k=k: lambda e: e.dma_start(
                    out=v32(AR1, 3072, DH), in_=W_out_d[h, k * 128:(k + 1) * 128, :]))(),
                    waits=[(DVE, S.v[DVE])], inc=LD, inc_n=16)
                S.emit("vector", (lambda h=h, k=k: lambda e: e.tensor_copy(
                    W_out_bf[:, (h * 2 + k) * DH:(h * 2 + k + 1) * DH],
                    v32(AR1, 3072, DH)))(), waits=[(LD, S.v[LD])], inc=DVE)
        S.emit("sync", lambda e: e.dma_start(out=v32(AR1, 5120, E, pdim=1), in_=b_rt_d),
               inc=LD, inc_n=16)
        S.emit("vector", lambda e: e.tensor_copy(b_rt_bf[:], v32(AR1, 5120, E, pdim=1)),
               waits=[(LD, S.v[LD])], inc=DVE)
        S.emit("sync", lambda e: e.dma_start(out=v32(AR1, 6144, D, pdim=E), in_=W_rt_d),
               inc=LD, inc_n=16)
        S.emit("vector", lambda e: e.tensor_copy(
            vbf(AR1, 7168, D, pdim=E), v32(AR1, 6144, D, pdim=E)),
            waits=[(LD, S.v[LD])], inc=DVE)
        for k in range(KC):
            S.emit("tensor", (lambda k=k: lambda e: e.transpose(
                out=PBF[:, 0:E], in_=vbf(AR1, 7168, D, pdim=E)[:, k * 128:(k + 1) * 128],
                identity=idm[0:E, 0:E]))(),
                waits=[(DVE, S.v[DVE]), (ACT, S.v[ACT])], inc=PE)
            S.emit("scalar", (lambda k=k: lambda e: e.copy(
                out=W_rtT_bf[:, k * E:(k + 1) * E], in_=PBF[:, 0:E]))(),
                waits=[(PE, S.v[PE])], inc=ACT)
        for g in range(G):
            h, dh = g // 2, g % 2
            S.emit("sync", (lambda g=g, h=h, dh=dh: lambda e: e.dma_start(
                out=conv_w_s[:, g * 4:(g + 1) * 4],
                in_=conv_w_d[h, dh * 128:(dh + 1) * 128, :]))(), inc=LD, inc_n=16)
            S.emit("sync", (lambda g=g, h=h, dh=dh: lambda e: e.dma_start(
                out=conv_b_s[:, g:g + 1], in_=conv_b_d[h, dh * 128:(dh + 1) * 128]
                .unsqueeze(1), allow_slow_non_contiguous=True))(), inc=LD, inc_n=16)
            S.emit("sync", (lambda g=g, h=h, dh=dh: lambda e: e.dma_start(
                out=Dp_s[:, g:g + 1], in_=Dp_d[h, dh * 128:(dh + 1) * 128]
                .unsqueeze(1), allow_slow_non_contiguous=True))(), inc=LD, inc_n=16)
            S.emit("sync", (lambda g=g, h=h, dh=dh: lambda e: e.dma_start(
                out=b_dt_s[:, g:g + 1], in_=b_dt_d[h, dh * 128:(dh + 1) * 128]
                .unsqueeze(1), allow_slow_non_contiguous=True))(), inc=LD, inc_n=16)
            src = A_log_d[:, g * NS:(g + 1) * NS, :]
            S.emit("sync", (lambda g=g, src=src: lambda e: e.dma_start(
                out=v32(AR1, 8192 + g * NS, NS), in_=src))(), inc=LD, inc_n=16)
        S.emit("scalar", lambda e: e.activation(
            out=v32(AR1, 9216, G * NS), in_=v32(AR1, 8192, G * NS), func=AF.Exp),
            waits=[(LD, S.v[LD])], inc=ACT)
        S.emit("vector", lambda e: e.tensor_scalar_mul(
            A_neg[:], v32(AR1, 9216, G * NS), -1.0),
            waits=[(ACT, S.v[ACT])], inc=DVE)
        S.emit("vector", lambda e: e.memset(ones_bf[:], 1.0), inc=DVE)
        S.emit("vector", lambda e: e.memset(carry[:], 0.0), inc=DVE)
        S.emit("vector", lambda e: e.memset(small[:, 6:7], 1e-5), inc=DVE)
        S.emit("vector", lambda e: e.memset(small[:, 7:8], 1.0), inc=DVE)

        def emit_ln_tile(i, xnslot, evac_to_xt=True):
            para_save = S.paranoid
            if isinstance(paranoid, (set, frozenset, list, tuple)):
                if "lnA" in paranoid:
                    S.paranoid = True
            xi = xtile(i)
            c0, c1, c2, c3 = (small[:, 0:1], small[:, 1:2], small[:, 2:3],
                              small[:, 3:4])
            S.emit("vector", lambda e: e.reduce_sum(c0, xi, axis=AX.X),
                   waits=[(ACT, S.v[ACT]), (LD, S.v[LD]), (GD, S.v[GD])], inc=DVE)
            S.emit("scalar", lambda e: e.activation(out=SQ, in_=xi, func=AF.Square),
                   waits=[(DVE, S.v[DVE])], inc=ACT)
            S.emit("vector", lambda e: e.reduce_sum(c1, SQ, axis=AX.X),
                   waits=[(ACT, S.v[ACT]), (DVE, S.v[DVE])], inc=DVE)
            S.emit("vector", lambda e: e.tensor_scalar_mul(c0, c0, 1.0 / D),
                   waits=[(DVE, S.v[DVE])], inc=DVE)
            S.emit("vector", lambda e: e.tensor_tensor(out=c2, in0=c0, in1=c0,
                                                       op=ALU.mult),
                   waits=[(DVE, S.v[DVE])], inc=DVE)
            S.emit("vector", lambda e: e.scalar_tensor_tensor(
                out=c1, in0=c1, scalar=1.0 / D, in1=c2, op0=ALU.mult,
                op1=ALU.subtract), waits=[(DVE, S.v[DVE])], inc=DVE)
            S.emit("scalar", lambda e: e.activation(out=c2, in_=c1, func=AF.Ln,
                                                    bias=small[:, 6:7]),
                   waits=[(DVE, S.v[DVE]), (ACT, S.v[ACT])], inc=ACT)
            S.emit("scalar", lambda e: e.activation(out=c3, in_=c2, func=AF.Exp,
                                                    scale=-0.5),
                   waits=[(ACT, S.v[ACT])], inc=ACT)
            if isinstance(paranoid, (set, frozenset, list, tuple)):
                S.paranoid = ("lnB" in paranoid) or para_save
            xn = XNORM[xnslot]
            S.emit("vector", lambda e: e.tensor_scalar(
                out=xn, in0=xi, scalar1=c0, scalar2=c3, op0=ALU.subtract,
                op1=ALU.mult), waits=[(ACT, S.v[ACT]), (DVE, S.v[DVE])], inc=DVE)
            dvv = S.v[DVE]
            for k in range(KC):
                S.emit("tensor", (lambda k=k, xn=xn: lambda e: e.transpose(
                    out=PBF[:, k * 128:(k + 1) * 128],
                    in_=xn[:, k * 128:(k + 1) * 128], identity=idm[:]))(),
                    waits=[(DVE, dvv), (ACT, S.v[ACT])], inc=PE)
            if evac_to_xt:
                dst = XT[:].rearrange("p (k t) -> p k t", k=KC)[:, :, i * P:(i + 1) * P]
                S.emit("scalar", (lambda dst=dst: lambda e: e.copy(
                    out=dst, in_=PBF[:].rearrange("p (k t) -> p k t", k=KC)))(),
                    waits=[(PE, S.v[PE])], inc=ACT)
            else:
                S.emit("scalar", lambda e: e.copy(out=PACK, in_=PBF[:]),
                       waits=[(PE, S.v[PE])], inc=ACT)
                for k in range(KC):
                    S.emit("gpsimd", (lambda k=k, i=i: lambda e: e.dma_start(
                        out=xT2_scr[k, :, i * P:(i + 1) * P],
                        in_=PACK[:, k * 128:(k + 1) * 128]))(),
                        waits=[(ACT, S.v[ACT])], inc=GD, inc_n=16)
            S.paranoid = para_save

        def emit_ffn(call):
            xt_done = S.v[ACT]
            S.emit("sync", lambda e: e.dma_start(out=SQ[0:E, :], in_=b_ex_d),
                   waits=[(DVE, S.v[DVE]), (ACT, S.v[ACT])], inc=LD, inc_n=16)
            S.emit("vector", lambda e: e.tensor_copy(b_ex_bf, SQ[0:E, :]),
                   waits=[(LD, S.v[LD])], inc=DVE)
            bexv = S.v[DVE]
            ps4_ready = 0
            for i in range(NT):
                for k in range(KC):
                    S.emit("tensor", (lambda i=i, k=k: lambda e: e.matmul(
                        out=PS[4][:, 0:E],
                        lhsT=XT[:, k * T + i * P: k * T + (i + 1) * P],
                        rhs=W_rtT_bf[:, k * E:(k + 1) * E],
                        start=(k == 0), stop=False))(),
                        waits=[(ACT, max(xt_done, ps4_ready))], inc=PE)
                S.emit("tensor", (lambda i=i: lambda e: e.matmul(
                    out=PS[4][:, 0:E], lhsT=ones_bf[:], rhs=b_rt_bf[:],
                    start=False, stop=True))(), inc=PE)
                S.emit("scalar", lambda e: e.copy(out=RL, in_=PS[4][:, 0:E]),
                       waits=[(PE, S.v[PE])], inc=ACT)
                ps4_ready = S.v[ACT]
                c0, c1 = small[:, 4:5], small[:, 5:6]
                S.emit("vector", lambda e: e.reduce_max(c0, RL, axis=AX.X),
                       waits=[(ACT, S.v[ACT])], inc=DVE)
                S.emit("vector", lambda e: e.tensor_scalar_mul(c0, c0, -1.0), inc=DVE)
                S.emit("scalar", lambda e: e.activation(out=RW[0], in_=RL,
                                                        func=AF.Exp, bias=c0),
                       waits=[(DVE, S.v[DVE])], inc=ACT)
                S.emit("vector", lambda e: e.reduce_sum(c1, RW[0], axis=AX.X),
                       waits=[(ACT, S.v[ACT])], inc=DVE)
                S.emit("vector", lambda e: e.reciprocal(c1, c1), inc=DVE)
                S.emit("vector", lambda e: e.tensor_scalar_mul(RW[1], RW[0], c1),
                       inc=DVE)
                S.emit("vector", lambda e: e.reduce_max(c0, RW[1], axis=AX.X), inc=DVE)
                S.emit("vector", lambda e: e.tensor_single_scalar(
                    RW[2], RW[1], c0, ALU.is_equal), inc=DVE)
                S.emit("vector", lambda e: e.scalar_tensor_tensor(
                    out=RW[2], in0=RW[2], scalar=-1e30, in1=RW[1], op0=ALU.mult,
                    op1=ALU.add), inc=DVE)
                S.emit("vector", lambda e: e.reduce_max(c0, RW[2], axis=AX.X), inc=DVE)
                S.emit("vector", lambda e: e.tensor_single_scalar(
                    RW[3], RW[1], c0, ALU.is_ge), inc=DVE)
                S.emit("vector", (lambda i=i: lambda e: e.scalar_tensor_tensor(
                    out=w_all[:, i * E:(i + 1) * E], in0=RW[1], scalar=0.5,
                    in1=RW[3], op0=ALU.mult, op1=ALU.mult))(), inc=DVE)
            for i in range(NT):
                S.emit("vector", (lambda i=i: lambda e: e.tensor_copy(
                    vbf(AR1, 13872, E), w_all[:, i * E:(i + 1) * E]))(),
                    waits=[(ACT, S.v[ACT])], inc=DVE)
                S.emit("tensor", (lambda i=i: lambda e: e.transpose(
                    out=PBF[0:E, 0:P], in_=vbf(AR1, 13872, E),
                    identity=idm[:]))(), waits=[(DVE, S.v[DVE])], inc=PE)
                S.emit("scalar", (lambda i=i: lambda e: e.copy(
                    out=wT8_bf[:, i * P:(i + 1) * P], in_=PBF[0:E, 0:P]))(),
                    waits=[(PE, S.v[PE])], inc=ACT)
            bias_ready = bexv
            wT8_done = S.v[ACT]
            for i in range(NT):
                for nt in range(2):
                    S.emit("tensor", (lambda i=i, nt=nt: lambda e: e.matmul(
                        out=PS[4 + nt][:], lhsT=wT8_bf[:, i * P:(i + 1) * P],
                        rhs=b_ex_bf[:, nt * 512:(nt + 1) * 512],
                        start=True, stop=True))(),
                        waits=[(ACT, wT8_done), (DVE, bias_ready)], inc=PE)
                pev = S.v[PE]
                for nt in range(2):
                    S.emit("vector", (lambda i=i, nt=nt: lambda e: e.tensor_add(
                        xtile(i)[:, nt * 512:(nt + 1) * 512], PS[4 + nt][:],
                        xtile(i)[:, nt * 512:(nt + 1) * 512]))(),
                        waits=[(PE, pev)], inc=DVE)
                bias_ready = S.v[DVE]
            bank_ready = [0, 0, 0, 0]
            wt_free = [0, 0]
            stg_free = [0, 0]
            stgb_free = [0, 0]
            for e_i in range(E):
                slot = e_i % 2
                if call == 0:
                    for ot in range(KC):
                        stg = WSTG[ot % 2]
                        stgb = WSTG_BF[ot % 2]
                        S.emit("sync", (lambda e_i=e_i, ot=ot, stg=stg:
                                        lambda e: e.dma_start(
                            out=stg, in_=W_ex_d[e_i, ot * 128:(ot + 1) * 128, :]))(),
                            waits=[(DVE, stg_free[ot % 2])], inc=LD, inc_n=16)
                        S.emit("vector", (lambda stg=stg, stgb=stgb:
                                          lambda e: e.tensor_copy(stgb, stg))(),
                            waits=[(LD, S.v[LD]), (PE, stgb_free[ot % 2])],
                            inc=DVE)
                        stg_free[ot % 2] = S.v[DVE]
                        dvv = S.v[DVE]
                        for k in range(KC):
                            S.emit("tensor", (lambda k=k, stgb=stgb:
                                              lambda e: e.transpose(
                                out=PBF[:, k * 128:(k + 1) * 128],
                                in_=stgb[:, k * 128:(k + 1) * 128],
                                identity=idm[:]))(),
                                waits=[(DVE, dvv), (ACT, S.v[ACT])], inc=PE)
                        stgb_free[ot % 2] = S.v[PE]
                        dst = WTv(slot).rearrange("p (k o) -> p k o", k=KC)[
                            :, :, ot * 128:(ot + 1) * 128]
                        S.emit("scalar", (lambda dst=dst: lambda e: e.copy(
                            out=dst,
                            in_=PBF[:].rearrange("p (k b) -> p k b", k=KC)))(),
                            waits=[(PE, S.v[PE])], inc=ACT)
                    S.emit("gpsimd", (lambda e_i=e_i, slot=slot:
                                      lambda e: e.dma_start(
                        out=wT_scr[e_i], in_=WTv(slot)))(),
                        waits=[(ACT, S.v[ACT])], inc=GD, inc_n=16)
                    wt_ok = [(ACT, S.v[ACT])]
                else:
                    S.emit("sync", (lambda e_i=e_i, slot=slot:
                                    lambda e: e.dma_start(
                        out=WTv(slot), in_=wT_scr[e_i]))(),
                        waits=[(PE, wt_free[slot])], inc=LD, inc_n=16)
                    wt_ok = [(LD, S.v[LD])]
                for i in range(NT):
                    for nt in range(2):
                        qi = slot * 2 + nt
                        q = PS[qi]
                        for k in range(KC):
                            S.emit("tensor", (lambda i=i, nt=nt, k=k, q=q, slot=slot:
                                              lambda e: e.matmul(
                                out=q[:],
                                lhsT=XT[:, k * T + i * P: k * T + (i + 1) * P],
                                rhs=WTv(slot).rearrange("p (k o) -> p k o", k=KC)[
                                    :, k, nt * 512:(nt + 1) * 512],
                                start=(k == 0), stop=(k == KC - 1)))(),
                                waits=wt_ok + [(DVE, bank_ready[qi])], inc=PE)
                        pev = S.v[PE]
                        S.emit("vector", (lambda i=i, nt=nt, q=q, e_i=e_i:
                                          lambda e: e.scalar_tensor_tensor(
                            out=xtile(i)[:, nt * 512:(nt + 1) * 512], in0=q[:],
                            scalar=w_all[:, i * E + e_i:i * E + e_i + 1],
                            in1=xtile(i)[:, nt * 512:(nt + 1) * 512],
                            op0=ALU.mult, op1=ALU.add))(),
                            waits=[(PE, pev)], inc=DVE)
                        bank_ready[qi] = S.v[DVE]
                wt_free[slot] = S.v[PE]

        set_para("ln")
        for i in range(NT):
            emit_ln_tile(i, i % 2, evac_to_xt=True)
        set_para("-")
        if upto == "ln1":
            S.emit("vector", lambda e: e.tensor_copy(v32(AR1, 0, 2048),
                                                     XT[:, 0:2048]),
                   waits=[(ACT, S.v[ACT])], inc=DVE)
            S.emit("gpsimd", lambda e: e.dma_start(
                out=x_scr[0:2].rearrange("a p d -> p a d"),
                in_=v32(AR1, 0, 2048)), waits=[(DVE, S.v[DVE])],
                inc=GD, inc_n=16)
            for i in range(NT):
                S.emit("gpsimd", (lambda i=i: lambda e: e.dma_start(
                    out=out_d[i * P:(i + 1) * P, :], in_=xtile(i)))(),
                    waits=[(DVE, S.v[DVE]), (ACT, S.v[ACT])], inc=GD, inc_n=16)
            S.emit("gpsimd", lambda e: e.nop(), waits=[(GD, S.v[GD])])
            S.finalize(block)
            return nc
        emit_ffn(0)
        for i in range(NT):
            emit_ln_tile(i, i % 2, evac_to_xt=False)
            S.emit("gpsimd", (lambda i=i: lambda e: e.dma_start(
                out=x_scr[i], in_=xtile(i)))(),
                waits=[(DVE, S.v[DVE])], inc=GD, inc_n=16)

        if upto == "ffn1":
            for i in range(NT):
                S.emit("gpsimd", (lambda i=i: lambda e: e.dma_start(
                    out=out_d[i * P:(i + 1) * P, :], in_=xtile(i)))(),
                    waits=[(DVE, S.v[DVE]), (ACT, S.v[ACT])], inc=GD, inc_n=16)
            if debug_scr:
                S.emit("gpsimd", lambda e: e.dma_start(
                    out=x_scr[0, :, 0:NT * E], in_=w_all[:, :]),
                    waits=[(DVE, S.v[DVE])], inc=GD, inc_n=16)
                S.emit("vector", lambda e: e.tensor_copy(SQ, XT[:, 0:1024]),
                       inc=DVE)
                S.emit("gpsimd", lambda e: e.dma_start(
                    out=x_scr[1], in_=SQ), waits=[(DVE, S.v[DVE])],
                    inc=GD, inc_n=16)
            S.emit("gpsimd", lambda e: e.nop(), waits=[(GD, S.v[GD])])
            S.finalize(block)
            return nc

        for g2 in range(G):
            S.emit("vector", (lambda g2=g2: lambda e: e.memset(
                xinv(g2, 4, off=0), 0.0))(),
                waits=[(GD, S.v[GD]), (ACT, S.v[ACT])], inc=DVE)
        for c in range(NCH):
            for h in range(H):
                S.emit("sync", (lambda c=c, h=h: lambda e: e.dma_start(
                    out=XT2C[c % 2].rearrange("p (h t) -> p h t", h=H)[:, h, :],
                    in_=xT2_scr[h, :, c * C:(c + 1) * C]))(),
                    waits=[(PE, S.v[PE]), (GD, S.v[GD])], inc=LD, inc_n=16)
            for s_ in range(2):
                S.emit("gpsimd", (lambda c=c, s_=s_: lambda e: e.dma_start(
                    out=XRES[s_], in_=x_scr[2 * c + s_]))(),
                    waits=[(DVE, S.v[DVE])], inc=GD, inc_n=16)
            ld_xt2 = S.v[LD]
            for h in range(H):
                pz_a, pz_b = PS[(h % 2) * 2], PS[(h % 2) * 2 + 1]
                rhs = XT2C[c % 2].rearrange("p (hh t) -> p hh t", hh=H)[:, h, :]
                for mi in range(4):
                    q = pz_a if mi < 2 else pz_b
                    col = (mi % 2) * 256
                    S.emit("tensor", (lambda h=h, mi=mi, q=q, col=col, rhs=rhs:
                                      lambda e: e.matmul(
                        out=q[:, col:col + 256],
                        lhsT=W_in_bf[:, h * 512 + mi * 128: h * 512 + (mi + 1) * 128],
                        rhs=rhs, start=True, stop=True))(),
                        waits=[(LD, ld_xt2), (ACT, S.v[ACT]), (DVE, S.v[DVE])],
                        inc=PE)
                pev = S.v[PE]
                for dh in range(2):
                    g2 = 2 * h + dh
                    S.emit("scalar", (lambda g2=g2, dh=dh, pz_a=pz_a:
                                      lambda e: e.copy(
                        out=xinv(g2, C, off=4), in_=pz_a[:, dh * 256:(dh + 1) * 256]))(),
                        waits=[(PE, pev), (DVE, S.v[DVE])], inc=ACT)
                    S.emit("scalar", (lambda g2=g2, dh=dh, pz_b=pz_b:
                                      lambda e: e.activation(
                        out=szv(g2), in_=pz_b[:, dh * 256:(dh + 1) * 256],
                        func=AF.Silu))(), inc=ACT)
                actv = S.v[ACT]
                for dh in range(2):
                    g2 = 2 * h + dh
                    acc = CONVACC[dh]
                    S.emit("vector", (lambda g2=g2, acc=acc: lambda e: e.tensor_scalar(
                        out=acc, in0=xinv(g2, C, off=1),
                        scalar1=conv_w_s[:, g2 * 4:g2 * 4 + 1],
                        scalar2=conv_b_s[:, g2:g2 + 1],
                        op0=ALU.mult, op1=ALU.add))(),
                        waits=[(ACT, actv)], inc=DVE)
                    for j in range(1, 4):
                        S.emit("vector", (lambda g2=g2, j=j, acc=acc:
                                          lambda e: e.scalar_tensor_tensor(
                            out=acc, in0=xinv(g2, C, off=1 + j),
                            scalar=conv_w_s[:, g2 * 4 + j:g2 * 4 + j + 1],
                            in1=acc, op0=ALU.mult, op1=ALU.add))(), inc=DVE)
                    S.emit("vector", (lambda g2=g2: lambda e: e.tensor_copy(
                        xinv(g2, 4, off=0), xinv(g2, 4, off=C)))(), inc=DVE)
                    dvv = S.v[DVE]
                    S.emit("scalar", (lambda g2=g2, acc=acc: lambda e: e.activation(
                        out=xinv(g2, C, off=4), in_=acc, func=AF.Silu))(),
                        waits=[(DVE, dvv)], inc=ACT)
                    S.emit("vector", (lambda g2=g2: lambda e: e.tensor_copy(
                        xcbfv(g2), xinv(g2, C, off=4)))(),
                        waits=[(ACT, S.v[ACT])], inc=DVE)
            for h in range(H):
                xc0, xc1 = xcbfv(2 * h), xcbfv(2 * h + 1)
                mm = [(PS[4], 0, 0, DTR), (PS[4], 256, DTR, DTR + NS),
                      (PS[5], 0, DTR + NS, DTR + 2 * NS)]
                for (q, col, m0, m1) in mm:
                    for k in range(2):
                        rhs = xc0 if k == 0 else xc1
                        S.emit("tensor", (lambda q=q, col=col, m0=m0, m1=m1, k=k,
                                          rhs=rhs, h=h: lambda e: e.matmul(
                            out=q[0:m1 - m0, col:col + 256],
                            lhsT=W_xp_bf[:, h * 80 + k * 40 + m0:
                                         h * 80 + k * 40 + m1],
                            rhs=rhs, start=(k == 0), stop=(k == 1)))(),
                            waits=[(DVE, S.v[DVE]), (ACT, S.v[ACT])], inc=PE)
                pev = S.v[PE]
                S.emit("scalar", lambda e: e.copy(out=dtr_bf[:],
                                                  in_=PS[4][0:DTR, 0:256]),
                       waits=[(PE, pev), (GD, S.v[GD])], inc=ACT)
                S.emit("scalar", lambda e: e.copy(out=bcs[0:NS, 0:C],
                                                  in_=PS[4][0:NS, 256:512]),
                       inc=ACT)
                S.emit("scalar", lambda e: e.copy(out=bcs[0:NS, C:2 * C],
                                                  in_=PS[5][0:NS, 0:256]), inc=ACT)
                actv = S.v[ACT]
                S.emit("gpsimd", (lambda h=h, c=c: lambda e: e.dma_start(
                    out=bc_scr[h, 0, :, c * C:(c + 1) * C], in_=bcs[0:NS, 0:C]))(),
                    waits=[(ACT, actv)], inc=GD, inc_n=16)
                S.emit("gpsimd", (lambda h=h, c=c: lambda e: e.dma_start(
                    out=bc_scr[h, 1, :, c * C:(c + 1) * C],
                    in_=bcs[0:NS, C:2 * C]))(), waits=[(ACT, actv)], inc=GD,
                    inc_n=16)
                for m in range(2):
                    S.emit("tensor", (lambda h=h, m=m: lambda e: e.matmul(
                        out=PS[3][:, m * 256:(m + 1) * 256],
                        lhsT=W_dt_bf[:, h * DI + m * 128: h * DI + (m + 1) * 128],
                        rhs=dtr_bf[:], start=True, stop=True))(),
                        waits=[(ACT, actv), (DVE, S.v[DVE])], inc=PE)
                pev = S.v[PE]
                for dh in range(2):
                    g2 = 2 * h + dh
                    S.emit("scalar", (lambda g2=g2, dh=dh: lambda e: e.copy(
                        out=SPX[0], in_=PS[3][:, dh * 256:(dh + 1) * 256],
                        )) (), waits=[(PE, pev), (DVE, S.v[DVE])], inc=ACT)
                    S.emit("vector", (lambda g2=g2: lambda e: e.tensor_scalar_add(
                        SPX[0], SPX[0], b_dt_s[:, g2:g2 + 1]))(),
                        waits=[(ACT, S.v[ACT])], inc=DVE)
                    S.emit("vector", lambda e: e.scalar_tensor_tensor(
                        out=SPX[1], in0=SPX[0], scalar=-1.0, in1=SPX[0],
                        op0=ALU.mult, op1=ALU.max), inc=DVE)
                    S.emit("scalar", lambda e: e.activation(
                        out=SPX[2], in_=SPX[1], func=AF.Exp, scale=-1.0),
                        waits=[(DVE, S.v[DVE])], inc=ACT)
                    S.emit("scalar", lambda e: e.activation(
                        out=SPX[2], in_=SPX[2], func=AF.Ln,
                        bias=small[:, 7:8]), inc=ACT)
                    S.emit("vector", lambda e: e.tensor_scalar_max(
                        SPX[1], SPX[0], 0.0), waits=[(ACT, S.v[ACT])], inc=DVE)
                    S.emit("vector", (lambda g2=g2: lambda e: e.tensor_add(
                        dtv(g2), SPX[1], SPX[2]))(),
                        waits=[(GD, S.v[GD])], inc=DVE)
                    S.emit("vector", (lambda g2=g2, dh=dh: lambda e: e.tensor_mul(
                        UBLK[dh], dtv(g2), xinv(g2, C, off=4)))(), inc=DVE)
                    dvv = S.v[DVE]
                    S.emit("gpsimd", (lambda g2=g2, h=h, dh=dh, c=c:
                                      lambda e: e.dma_start(
                        out=dt_scr[h, dh * 128:(dh + 1) * 128, c * C:(c + 1) * C],
                        in_=dtv(g2)))(), waits=[(DVE, dvv)], inc=GD, inc_n=16)
                    S.emit("gpsimd", (lambda h=h, dh=dh, c=c, dh2=dh:
                                      lambda e: e.dma_start(
                        out=u_scr[h, dh * 128:(dh + 1) * 128, c * C:(c + 1) * C],
                        in_=UBLK[dh2]))(), waits=[(DVE, dvv)], inc=GD, inc_n=16)
            gd_bc = S.v[GD]
            brep3 = BREP.rearrange("p (n t) -> p n t", n=NS)
            crep3 = CREP.rearrange("p (n t) -> p n t", n=NS)
            dvv_bc = S.v[DVE]
            for h in range(H):
                srcB = bc_scr[h, 0, :, c * C:(c + 1) * C].unsqueeze(0)\
                    .broadcast_to([NS, NS, C])
                srcC = bc_scr[h, 1, :, c * C:(c + 1) * C].unsqueeze(0)\
                    .broadcast_to([NS, NS, C])
                S.emit("sync", (lambda h=h, srcB=srcB, brep3=brep3:
                                lambda e: e.dma_start(
                    out=brep3[h * NS:(h + 1) * NS], in_=srcB))(),
                    waits=[(GD, gd_bc), (DVE, dvv_bc)], inc=LD, inc_n=16)
                S.emit("sync", (lambda h=h, srcC=srcC, crep3=crep3:
                                lambda e: e.dma_start(
                    out=crep3[h * NS:(h + 1) * NS], in_=srcC))(),
                    waits=[(GD, gd_bc), (DVE, dvv_bc)], inc=LD, inc_n=16)
            ld_bc = S.v[LD]
            gd_dtu = S.v[GD]
            for g in range(G):
                sl = g % 2
                src_dt = dt_scr[:, g * NS:(g + 1) * NS, c * C:(c + 1) * C]
                src_u = u_scr[:, g * NS:(g + 1) * NS, c * C:(c + 1) * C]
                S.emit("sync", (lambda src_dt=src_dt, sl=sl: lambda e: e.dma_start(
                    out=DTI[sl], in_=src_dt))(),
                    waits=[(GD, gd_dtu), (ACT, S.v[ACT]), (DVE, S.v[DVE])],
                    inc=LD, inc_n=16)
                S.emit("sync", (lambda src_u=src_u, sl=sl: lambda e: e.dma_start(
                    out=UI[sl], in_=src_u))(),
                    waits=[(GD, gd_dtu), (DVE, S.v[DVE])], inc=LD, inc_n=16)
                ldv = S.v[LD]
                da3 = DA[sl].rearrange("p (n t) -> p n t", n=NS)
                S.emit("vector", (lambda g=g, sl=sl, da3=da3: lambda e: e.tensor_tensor(
                    out=da3,
                    in0=DTI[sl].unsqueeze(1).broadcast_to([P, NS, C]),
                    in1=A_neg[:, g * NS:(g + 1) * NS].unsqueeze(2)
                    .broadcast_to([P, NS, C]), op=ALU.mult))(),
                    waits=[(LD, ldv), (ACT, S.v[ACT])], inc=DVE)
                S.emit("scalar", (lambda sl=sl: lambda e: e.activation(
                    out=DA[sl], in_=DA[sl], func=AF.Exp))(),
                    waits=[(DVE, S.v[DVE])], inc=ACT)
                dbx3 = DBX[sl].rearrange("p (n t) -> p n t", n=NS)
                S.emit("vector", (lambda sl=sl, dbx3=dbx3: lambda e: e.tensor_tensor(
                    out=dbx3,
                    in0=UI[sl].unsqueeze(1).broadcast_to([P, NS, C]),
                    in1=BREP.rearrange("p (n t) -> p n t", n=NS),
                    op=ALU.mult))(), waits=[(LD, ld_bc)], inc=DVE)
                da0 = DA[sl].rearrange("p (n t) -> p n t", n=NS)[:, :, 0]
                db0 = DBX[sl].rearrange("p (n t) -> p n t", n=NS)[:, :, 0]
                tmpv = small[:, 16:32]
                S.emit("vector", (lambda g=g, da0=da0, tmpv=tmpv:
                                  lambda e: e.tensor_mul(
                    tmpv, da0, carry[:, g * NS:(g + 1) * NS]))(),
                    waits=[(ACT, S.v[ACT])], inc=DVE)
                S.emit("vector", (lambda da0=da0: lambda e: e.memset(da0, 0.0))(),
                       inc=DVE)
                S.emit("vector", (lambda db0=db0, tmpv=tmpv:
                                  lambda e: e.tensor_add(db0, tmpv, db0))(), inc=DVE)
                S.emit("vector", (lambda sl=sl: lambda e: e.tensor_tensor_scan(
                    out=DBX[sl], data0=DA[sl], data1=DBX[sl], initial=0.0,
                    op0=ALU.mult, op1=ALU.add))(), inc=DVE)
                hv3 = DBX[sl].rearrange("p (n t) -> p n t", n=NS)
                S.emit("vector", (lambda g=g, hv3=hv3: lambda e: e.tensor_copy(
                    carry[:, g * NS:(g + 1) * NS], hv3[:, :, C - 1]))(), inc=DVE)
                S.emit("vector", (lambda sl=sl: lambda e: e.tensor_tensor(
                    out=DA[sl], in0=DBX[sl], in1=CREP, op=ALU.mult))(), inc=DVE)
                width = NS
                while width > 1:
                    half = width // 2
                    a = DA[sl][:, 0:half * C]
                    b = DA[sl][:, half * C:width * C]
                    S.emit("vector", (lambda a=a, b=b: lambda e: e.tensor_add(
                        a, a, b))(), inc=DVE)
                    width = half
                S.emit("vector", (lambda sl=sl: lambda e: e.tensor_copy(
                    YI[sl], DA[sl][:, 0:C]))(), waits=[(GD, S.v[GD])], inc=DVE)
                dvv = S.v[DVE]
                dst_y = y_scr[:, g * NS:(g + 1) * NS, c * C:(c + 1) * C]
                S.emit("gpsimd", (lambda dst_y=dst_y, sl=sl: lambda e: e.dma_start(
                    out=dst_y, in_=YI[sl]))(), waits=[(DVE, dvv)], inc=GD, inc_n=16)
            gd_y = S.v[GD]
            for h in range(H):
                for dh in range(2):
                    g2 = 2 * h + dh
                    S.emit("sync", (lambda h=h, dh=dh, c=c, g2=g2:
                                    lambda e: e.dma_start(
                        out=yblk(g2),
                        in_=y_scr[h, dh * 128:(dh + 1) * 128, c * C:(c + 1) * C]))(),
                        waits=[(GD, gd_y), (DVE, S.v[DVE])], inc=LD, inc_n=16)
                    ldv = S.v[LD]
                    S.emit("vector", (lambda g2=g2: lambda e: e.scalar_tensor_tensor(
                        out=yblk(g2), in0=xinv(g2, C, off=4),
                        scalar=Dp_s[:, g2:g2 + 1], in1=yblk(g2),
                        op0=ALU.mult, op1=ALU.add))(), waits=[(LD, ldv)], inc=DVE)
                    S.emit("vector", (lambda g2=g2: lambda e: e.tensor_mul(
                        ybfv(g2), yblk(g2), szv(g2)))(), inc=DVE)
                dvv = S.v[DVE]
                for tsub in range(2):
                    for k in range(2):
                        S.emit("tensor", (lambda h=h, tsub=tsub, k=k:
                                          lambda e: e.matmul(
                            out=PS[5][:, tsub * 128:(tsub + 1) * 128],
                            lhsT=ybfv(2 * h + k)[:, tsub * 128:(tsub + 1) * 128],
                            rhs=W_out_bf[:, (h * 2 + k) * DH:(h * 2 + k + 1) * DH],
                            start=(k == 0), stop=(k == 1)))(),
                            waits=[(DVE, dvv)], inc=PE)
                pev = S.v[PE]
                for tsub in range(2):
                    S.emit("vector", (lambda h=h, tsub=tsub: lambda e: e.tensor_add(
                        XRES[tsub][:, h * 128:(h + 1) * 128],
                        PS[5][:, tsub * 128:(tsub + 1) * 128],
                        XRES[tsub][:, h * 128:(h + 1) * 128]))(),
                        waits=[(PE, pev), (GD, S.v[GD])], inc=DVE)
            for s_ in range(2):
                S.emit("gpsimd", (lambda c=c, s_=s_: lambda e: e.dma_start(
                    out=x_scr[2 * c + s_], in_=XRES[s_]))(),
                    waits=[(DVE, S.v[DVE])], inc=GD, inc_n=16)

        gd_m = S.v[GD]
        for i in range(NT):
            S.emit("sync", (lambda i=i: lambda e: e.dma_start(
                out=xtile(i), in_=x_scr[i]))(),
                waits=[(GD, gd_m), (DVE, S.v[DVE]), (ACT, S.v[ACT]),
                       (PE, S.v[PE])], inc=LD, inc_n=16)
        if upto != "mamba":
            for i in range(NT):
                emit_ln_tile(i, i % 2, evac_to_xt=True)
            emit_ffn(1)
        for i in range(NT):
            S.emit("gpsimd", (lambda i=i: lambda e: e.dma_start(
                out=out_d[i * P:(i + 1) * P, :], in_=xtile(i)))(),
                waits=[(DVE, S.v[DVE]), (ACT, S.v[ACT])], inc=GD, inc_n=16)
        S.emit("gpsimd", lambda e: e.nop(), waits=[(GD, S.v[GD])])

        S.finalize(block)
    return nc


def kernel(L0_attn_x, L0_attn_W_in, L0_attn_conv_w, L0_attn_conv_b, L0_attn_W_xp,
           L0_attn_W_dt, L0_attn_b_dt, L0_attn_A_log, L0_attn_Dp, L0_attn_W_out,
           L0_ffn_W_experts, L0_ffn_b_experts, L0_ffn_W_router, L0_ffn_b_router):
    nc = build()
    idm = np.eye(P, dtype=ml_dtypes.bfloat16)
    base = dict(
        W_in=np.ascontiguousarray(L0_attn_W_in, np.float32),
        conv_w=np.ascontiguousarray(L0_attn_conv_w, np.float32),
        conv_b=np.ascontiguousarray(L0_attn_conv_b, np.float32),
        W_xp=np.ascontiguousarray(L0_attn_W_xp, np.float32),
        W_dt=np.ascontiguousarray(L0_attn_W_dt, np.float32),
        b_dt=np.ascontiguousarray(L0_attn_b_dt, np.float32),
        A_log=np.ascontiguousarray(L0_attn_A_log, np.float32),
        Dp=np.ascontiguousarray(L0_attn_Dp, np.float32),
        W_out=np.ascontiguousarray(L0_attn_W_out, np.float32),
        W_ex=np.ascontiguousarray(L0_ffn_W_experts, np.float32),
        b_ex=np.ascontiguousarray(L0_ffn_b_experts, np.float32),
        W_rt=np.ascontiguousarray(L0_ffn_W_router, np.float32),
        b_rt=np.ascontiguousarray(L0_ffn_b_router, np.float32).reshape(1, E),
        idm=idm,
    )
    x = np.ascontiguousarray(L0_attn_x, np.float32)
    in_maps = [dict(x=x[c], **base) for c in range(8)]
    res = run_bass_kernel_spmd(nc, in_maps, core_ids=list(range(8)))
    return np.stack([np.asarray(res.results[c]["out"]) for c in range(8)])


# version 07
# speedup vs baseline: 1.1070x; 1.1070x over previous
import numpy as np
import ml_dtypes
import contextlib
import concourse.bass as bass
import concourse.mybir as mybir
from concourse.bass_utils import run_bass_kernel_spmd

F32 = mybir.dt.float32
BF16 = mybir.dt.bfloat16
AF = mybir.ActivationFunctionType
ALU = mybir.AluOpType
AX = mybir.AxisListType

P = 128
T = 2048
D = 1024
NT = 16
E = 8
KC = 8
H = 8
DH = 128
DI = 256
NS = 16
DTR = 8
C = 256
NCH = T // C
G = 16

ENGINES = ("sync", "gpsimd", "vector", "scalar", "tensor")


class Sched:

    def __init__(self, sems, paranoid=False):
        self.ops = {e: [] for e in ENGINES}
        self.sems = sems
        self.v = {n: 0 for n in sems}
        self.paranoid = paranoid

    SELF = {"vector": "s_dve", "scalar": "s_act"}

    def emit(self, eng, fn, waits=(), inc=None, inc_n=1, drain=False):
        if self.paranoid:
            waits = [(n, v) for n, v in self.v.items()]
        else:
            own = self.SELF.get(eng)
            if own is not None:
                waits = list(waits) + [(own, self.v[own])]
        w = [(self.sems[s], v) for (s, v) in waits if v > 0]
        self.ops[eng].append((w, fn, (self.sems[inc], inc_n) if inc else None,
                              drain))
        if inc:
            self.v[inc] += inc_n

    def barrier_vals(self):
        return dict(self.v)

    def wait_all(self):
        return [(n, self.v[n]) for n in self.sems]

    def finalize(self, block):
        for eng in ENGINES:
            ops = self.ops[eng]
            if not ops:
                continue

            def make(eng_ops):
                def body(e):
                    seen = {}
                    for waits, fn, inc, drain in eng_ops:
                        for s, v in waits:
                            if seen.get(s.name, -1) < v:
                                e.wait_ge(s, v)
                                seen[s.name] = v
                        if drain:
                            e.drain()
                        ins = fn(e)
                        if inc is not None:
                            ins.then_inc(inc[0], inc[1])
                return body
            getattr(block, eng)(make(ops))


def build(debug_tap=None, debug_scr=False, upto="full", paranoid=False):
    nc = bass.Bass("TRN2", target_bir_lowering=False)
    scr_kind = "ExternalOutput" if debug_scr else "Internal"

    x_d = nc.dram_tensor("x", [T, D], F32, kind="ExternalInput").ap()
    W_in_d = nc.dram_tensor("W_in", [H, DH, 2 * DI], F32, kind="ExternalInput").ap()
    conv_w_d = nc.dram_tensor("conv_w", [H, DI, 4], F32, kind="ExternalInput").ap()
    conv_b_d = nc.dram_tensor("conv_b", [H, DI], F32, kind="ExternalInput").ap()
    W_xp_d = nc.dram_tensor("W_xp", [H, DI, DTR + 2 * NS], F32, kind="ExternalInput").ap()
    W_dt_d = nc.dram_tensor("W_dt", [H, DTR, DI], F32, kind="ExternalInput").ap()
    b_dt_d = nc.dram_tensor("b_dt", [H, DI], F32, kind="ExternalInput").ap()
    A_log_d = nc.dram_tensor("A_log", [H, DI, NS], F32, kind="ExternalInput").ap()
    Dp_d = nc.dram_tensor("Dp", [H, DI], F32, kind="ExternalInput").ap()
    W_out_d = nc.dram_tensor("W_out", [H, DI, DH], F32, kind="ExternalInput").ap()
    W_ex_d = nc.dram_tensor("W_ex", [E, D, D], F32, kind="ExternalInput").ap()
    b_ex_d = nc.dram_tensor("b_ex", [E, D], F32, kind="ExternalInput").ap()
    W_rt_d = nc.dram_tensor("W_rt", [E, D], F32, kind="ExternalInput").ap()
    b_rt_d = nc.dram_tensor("b_rt", [1, E], F32, kind="ExternalInput").ap()
    idm_d = nc.dram_tensor("idm", [P, P], BF16, kind="ExternalInput").ap()

    out_d = nc.dram_tensor("out", [T, D], F32, kind="ExternalOutput").ap()
    tap_d = None
    if debug_tap is not None:
        tap_d = nc.dram_tensor("tap", list(debug_tap), F32, kind="ExternalOutput").ap()

    x_scr = nc.dram_tensor("x_scr", [NT, P, D], F32, kind=scr_kind).ap()
    xT2_scr = nc.dram_tensor("xT2_scr", [KC, P, T], BF16, kind="Internal").ap()
    wT_scr = nc.dram_tensor("wT_scr", [E, P, KC * D], BF16, kind="Internal").ap()
    bc_scr = nc.dram_tensor("bc_scr", [H, 2, NS, T], F32, kind=scr_kind).ap()
    dt_scr = nc.dram_tensor("dt_scr", [H, DI, T], F32, kind=scr_kind).ap()
    u_scr = nc.dram_tensor("u_scr", [H, DI, T], F32, kind=scr_kind).ap()
    y_scr = nc.dram_tensor("y_scr", [H, DI, T], F32, kind=scr_kind).ap()

    with contextlib.ExitStack() as st:
        def sb(name, pdim, cols_f32):
            return st.enter_context(nc.sbuf_tensor(name, [pdim, cols_f32], F32))

        AR1 = sb("AR1", P, 20480)
        AR2 = sb("AR2", P, 16384)
        AR3 = sb("AR3", P, 8192)

        W_in_bf = st.enter_context(nc.sbuf_tensor("W_in_bf", [P, H * 512], BF16))
        W_xp_bf = st.enter_context(nc.sbuf_tensor("W_xp_bf", [P, H * 2 * 40], BF16))
        W_dt_bf = st.enter_context(nc.sbuf_tensor("W_dt_bf", [DTR, H * DI], BF16))
        W_out_bf = st.enter_context(nc.sbuf_tensor("W_out_bf", [P, H * 2 * DH], BF16))
        W_rtT_bf = st.enter_context(nc.sbuf_tensor("W_rtT_bf", [P, KC * E], BF16))
        b_rt_bf = st.enter_context(nc.sbuf_tensor("b_rt_bf", [1, E], BF16))
        idm = st.enter_context(nc.sbuf_tensor("idm_s", [P, P], BF16))
        ones_bf = st.enter_context(nc.sbuf_tensor("ones_bf", [1, P], BF16))
        A_neg = sb("A_neg", P, G * NS)
        conv_w_s = sb("conv_w_s", P, G * 4)
        conv_b_s = sb("conv_b_s", P, G)
        Dp_s = sb("Dp_s", P, G)
        b_dt_s = sb("b_dt_s", P, G)
        w_all = sb("w_all", P, NT * E)
        carry = sb("carry", P, G * NS)
        small = sb("small", P, 64)
        dtr_bf = st.enter_context(nc.sbuf_tensor("dtr_bf", [DTR, C], BF16))
        bcs = sb("bcs", P, 2 * C)

        PS = [st.enter_context(nc.psum_tensor(f"PS{i}", [P, 512], F32))
              for i in range(6)]
        PBF = st.enter_context(nc.psum_tensor("PBF", [P, 1024], BF16))

        semnames = ["s_ld", "s_gd", "s_pe", "s_act", "s_dve", "s_bc"]
        sems = {n: st.enter_context(nc.semaphore(n)) for n in semnames}
        block = st.enter_context(nc.Block())
        S = Sched(sems, paranoid=(paranoid is True))
        def set_para(phase):
            if isinstance(paranoid, (set, frozenset, list, tuple)):
                S.paranoid = phase in paranoid

        def v32(ar, off, cols, pdim=P):
            return ar[0:pdim, off:off + cols]

        def vbf(ar, off_f32, cols_bf, pdim=P):
            return ar[0:pdim].bitcast(BF16)[0:pdim, 2 * off_f32:2 * off_f32 + cols_bf]

        b_ex_bf = vbf(AR1, 13888, D, pdim=E)
        wT8_bf = vbf(AR1, 14400, T, pdim=E)

        def xtile(i):
            return v32(AR2, i * D, D)

        def WTv(slot):
            return vbf(AR1, slot * 4096, 8192)
        WSTG = [v32(AR1, 8192 + i * 1024, 1024) for i in range(2)]
        WSTG_BF = [vbf(AR1, 10240 + i * 512, 1024) for i in range(2)]
        XNORM = [vbf(AR1, 11264 + i * 512, 1024) for i in range(2)]
        PACK = vbf(AR1, 12288, 1024)
        SQ = v32(AR1, 12800, 1024)
        RL = v32(AR1, 13824, 8)
        RW = [v32(AR1, 13832 + i * 8, 8) for i in range(4)]

        XT = AR3[:].bitcast(BF16)

        DA = [v32(AR1, i * 4096, 4096) for i in range(2)]
        DBX = [v32(AR1, 8192, 4096)] * 2
        BREP = v32(AR1, 12288, 4096)
        CREP = v32(AR1, 16384, 4096)
        def xinv(g2, cols=C + 4, off=0):
            return v32(AR2, g2 * (C + 4) + off, cols)
        SZOFF = G * (C + 4)
        def szv(g2):
            return v32(AR2, SZOFF + g2 * C, C)
        DTOFF = SZOFF + G * C
        def dtv(g2):
            return v32(AR2, DTOFF + g2 * C, C)
        yblk = dtv
        XRES = [v32(AR2, 12352 + i * D, D) for i in range(2)]
        SPX = [v32(AR2, 14400 + i * C, C) for i in range(3)]
        CONVACC = [v32(AR2, 15168 + i * C, C) for i in range(2)]
        UBLK = [v32(AR2, 15680 + i * C, C) for i in range(2)]
        def xcbfv(g2):
            return vbf(AR3, g2 * 128, 256)
        def ybfv(g2):
            return vbf(AR3, 2048 + g2 * 128, 256)
        DTI = [v32(AR3, 4096 + i * C, C) for i in range(2)]
        UI = [v32(AR3, 4608 + i * C, C) for i in range(2)]
        YI = [v32(AR3, 5120 + i * C, C) for i in range(2)]
        XT2C = [vbf(AR3, 5632 + i * 1024, 2048) for i in range(2)]

        LD, GD, PE, ACT, DVE = "s_ld", "s_gd", "s_pe", "s_act", "s_dve"

        set_para("p0")
        for i in range(NT):
            S.emit("sync", (lambda i=i: lambda e: e.dma_start(
                out=xtile(i), in_=x_d[i * P:(i + 1) * P, :]))(), inc=LD, inc_n=16)
        S.emit("sync", lambda e: e.dma_start(out=idm[:], in_=idm_d), inc=LD, inc_n=16)
        for h in range(H):
            S.emit("sync", (lambda h=h: lambda e: e.dma_start(
                out=v32(AR1, 0, 512), in_=W_in_d[h]))(), inc=LD, inc_n=16)
            S.emit("vector", (lambda h=h: lambda e: e.tensor_copy(
                W_in_bf[:, h * 512:(h + 1) * 512], v32(AR1, 0, 512)))(),
                waits=[(LD, S.v[LD])], inc=DVE)
            for k in range(2):
                S.emit("sync", (lambda h=h, k=k: lambda e: e.dma_start(
                    out=v32(AR1, 1024, 40), in_=W_xp_d[h, k * 128:(k + 1) * 128, :]))(),
                    waits=[(DVE, S.v[DVE])], inc=LD, inc_n=16)
                S.emit("vector", (lambda h=h, k=k: lambda e: e.tensor_copy(
                    W_xp_bf[:, (h * 2 + k) * 40:(h * 2 + k + 1) * 40],
                    v32(AR1, 1024, 40)))(), waits=[(LD, S.v[LD])], inc=DVE)
            S.emit("sync", (lambda h=h: lambda e: e.dma_start(
                out=v32(AR1, 2048, DI, pdim=DTR), in_=W_dt_d[h]))(),
                waits=[(DVE, S.v[DVE])], inc=LD, inc_n=16)
            S.emit("vector", (lambda h=h: lambda e: e.tensor_copy(
                W_dt_bf[:, h * DI:(h + 1) * DI], v32(AR1, 2048, DI, pdim=DTR)))(),
                waits=[(LD, S.v[LD])], inc=DVE)
            for k in range(2):
                S.emit("sync", (lambda h=h, k=k: lambda e: e.dma_start(
                    out=v32(AR1, 3072, DH), in_=W_out_d[h, k * 128:(k + 1) * 128, :]))(),
                    waits=[(DVE, S.v[DVE])], inc=LD, inc_n=16)
                S.emit("vector", (lambda h=h, k=k: lambda e: e.tensor_copy(
                    W_out_bf[:, (h * 2 + k) * DH:(h * 2 + k + 1) * DH],
                    v32(AR1, 3072, DH)))(), waits=[(LD, S.v[LD])], inc=DVE)
        S.emit("sync", lambda e: e.dma_start(out=v32(AR1, 5120, E, pdim=1), in_=b_rt_d),
               inc=LD, inc_n=16)
        S.emit("vector", lambda e: e.tensor_copy(b_rt_bf[:], v32(AR1, 5120, E, pdim=1)),
               waits=[(LD, S.v[LD])], inc=DVE)
        S.emit("sync", lambda e: e.dma_start(out=v32(AR1, 6144, D, pdim=E), in_=W_rt_d),
               inc=LD, inc_n=16)
        S.emit("vector", lambda e: e.tensor_copy(
            vbf(AR1, 7168, D, pdim=E), v32(AR1, 6144, D, pdim=E)),
            waits=[(LD, S.v[LD])], inc=DVE)
        for k in range(KC):
            S.emit("tensor", (lambda k=k: lambda e: e.transpose(
                out=PBF[:, 0:E], in_=vbf(AR1, 7168, D, pdim=E)[:, k * 128:(k + 1) * 128],
                identity=idm[0:E, 0:E]))(),
                waits=[(DVE, S.v[DVE]), (ACT, S.v[ACT])], inc=PE)
            S.emit("scalar", (lambda k=k: lambda e: e.copy(
                out=W_rtT_bf[:, k * E:(k + 1) * E], in_=PBF[:, 0:E]))(),
                waits=[(PE, S.v[PE])], inc=ACT)
        for g in range(G):
            h, dh = g // 2, g % 2
            S.emit("sync", (lambda g=g, h=h, dh=dh: lambda e: e.dma_start(
                out=conv_w_s[:, g * 4:(g + 1) * 4],
                in_=conv_w_d[h, dh * 128:(dh + 1) * 128, :]))(), inc=LD, inc_n=16)
            S.emit("sync", (lambda g=g, h=h, dh=dh: lambda e: e.dma_start(
                out=conv_b_s[:, g:g + 1], in_=conv_b_d[h, dh * 128:(dh + 1) * 128]
                .unsqueeze(1), allow_slow_non_contiguous=True))(), inc=LD, inc_n=16)
            S.emit("sync", (lambda g=g, h=h, dh=dh: lambda e: e.dma_start(
                out=Dp_s[:, g:g + 1], in_=Dp_d[h, dh * 128:(dh + 1) * 128]
                .unsqueeze(1), allow_slow_non_contiguous=True))(), inc=LD, inc_n=16)
            S.emit("sync", (lambda g=g, h=h, dh=dh: lambda e: e.dma_start(
                out=b_dt_s[:, g:g + 1], in_=b_dt_d[h, dh * 128:(dh + 1) * 128]
                .unsqueeze(1), allow_slow_non_contiguous=True))(), inc=LD, inc_n=16)
            src = A_log_d[:, g * NS:(g + 1) * NS, :]
            S.emit("sync", (lambda g=g, src=src: lambda e: e.dma_start(
                out=v32(AR1, 8192 + g * NS, NS), in_=src))(), inc=LD, inc_n=16)
        S.emit("scalar", lambda e: e.activation(
            out=v32(AR1, 9216, G * NS), in_=v32(AR1, 8192, G * NS), func=AF.Exp),
            waits=[(LD, S.v[LD])], inc=ACT)
        S.emit("vector", lambda e: e.tensor_scalar_mul(
            A_neg[:], v32(AR1, 9216, G * NS), -1.0),
            waits=[(ACT, S.v[ACT])], inc=DVE)
        S.emit("vector", lambda e: e.memset(ones_bf[:], 1.0), inc=DVE)
        S.emit("vector", lambda e: e.memset(carry[:], 0.0), inc=DVE)
        S.emit("vector", lambda e: e.memset(small[:, 6:7], 1e-5), inc=DVE)
        S.emit("vector", lambda e: e.memset(small[:, 7:8], 1.0), inc=DVE)

        SQ2 = v32(AR1, 15424, 1024)
        MS = small[:, 32:48]
        QS = small[:, 48:64]
        M2S = v32(AR1, 12800, 16)

        def emit_ln_all(evac_to_xt=True):
            for i in range(NT):
                xi = xtile(i)
                sq = SQ if i % 2 == 0 else SQ2
                S.emit("vector", (lambda i=i, xi=xi: lambda e: e.reduce_sum(
                    MS[:, i:i + 1], xi, axis=AX.X))(),
                    waits=[(ACT, S.v[ACT]), (LD, S.v[LD]), (GD, S.v[GD])],
                    inc=DVE)
                S.emit("scalar", (lambda xi=xi, sq=sq: lambda e: e.activation(
                    out=sq, in_=xi, func=AF.Square))(),
                    waits=[(DVE, S.v[DVE])], inc=ACT)
                S.emit("vector", (lambda i=i, sq=sq: lambda e: e.reduce_sum(
                    QS[:, i:i + 1], sq, axis=AX.X))(),
                    waits=[(ACT, S.v[ACT])], inc=DVE)
            S.emit("vector", lambda e: e.tensor_scalar_mul(MS, MS, 1.0 / D),
                   inc=DVE)
            S.emit("vector", lambda e: e.tensor_tensor(out=M2S, in0=MS, in1=MS,
                                                       op=ALU.mult), inc=DVE)
            S.emit("vector", lambda e: e.scalar_tensor_tensor(
                out=QS, in0=QS, scalar=1.0 / D, in1=M2S, op0=ALU.mult,
                op1=ALU.subtract), inc=DVE)
            S.emit("scalar", lambda e: e.activation(out=QS, in_=QS, func=AF.Ln,
                                                    bias=small[:, 6:7]),
                   waits=[(DVE, S.v[DVE])], inc=ACT)
            S.emit("scalar", lambda e: e.activation(out=QS, in_=QS, func=AF.Exp,
                                                    scale=-0.5), inc=ACT)
            for i in range(NT):
                xi = xtile(i)
                xn = XNORM[i % 2]
                S.emit("vector", (lambda i=i, xi=xi, xn=xn:
                                  lambda e: e.tensor_scalar(
                    out=xn, in0=xi, scalar1=MS[:, i:i + 1],
                    scalar2=QS[:, i:i + 1], op0=ALU.subtract, op1=ALU.mult))(),
                    waits=[(ACT, S.v[ACT])], inc=DVE)
                dvv = S.v[DVE]
                for k in range(KC):
                    S.emit("tensor", (lambda k=k, xn=xn: lambda e: e.transpose(
                        out=PBF[:, k * 128:(k + 1) * 128],
                        in_=xn[:, k * 128:(k + 1) * 128], identity=idm[:]))(),
                        waits=[(DVE, dvv), (ACT, S.v[ACT])], inc=PE)
                if evac_to_xt:
                    dst = XT[:].rearrange("p (k t) -> p k t", k=KC)[
                        :, :, i * P:(i + 1) * P]
                    S.emit("scalar", (lambda dst=dst: lambda e: e.copy(
                        out=dst, in_=PBF[:].rearrange("p (k t) -> p k t",
                                                      k=KC)))(),
                        waits=[(PE, S.v[PE])], inc=ACT)
                else:
                    S.emit("scalar", lambda e: e.copy(out=PACK, in_=PBF[:]),
                           waits=[(PE, S.v[PE]), (GD, S.v[GD])], inc=ACT)
                    for k in range(KC):
                        S.emit("gpsimd", (lambda k=k, i=i: lambda e: e.dma_start(
                            out=xT2_scr[k, :, i * P:(i + 1) * P],
                            in_=PACK[:, k * 128:(k + 1) * 128]))(),
                            waits=[(ACT, S.v[ACT])], inc=GD, inc_n=16)

        def emit_ffn(call):
            xt_done = S.v[ACT]
            S.emit("sync", lambda e: e.dma_start(out=SQ[0:E, :], in_=b_ex_d),
                   waits=[(DVE, S.v[DVE]), (ACT, S.v[ACT])], inc=LD, inc_n=16)
            S.emit("vector", lambda e: e.tensor_copy(b_ex_bf, SQ[0:E, :]),
                   waits=[(LD, S.v[LD])], inc=DVE)
            bexv = S.v[DVE]
            ps4_ready = 0
            for i in range(NT):
                for k in range(KC):
                    S.emit("tensor", (lambda i=i, k=k: lambda e: e.matmul(
                        out=PS[4][:, 0:E],
                        lhsT=XT[:, k * T + i * P: k * T + (i + 1) * P],
                        rhs=W_rtT_bf[:, k * E:(k + 1) * E],
                        start=(k == 0), stop=False))(),
                        waits=[(ACT, max(xt_done, ps4_ready))], inc=PE)
                S.emit("tensor", (lambda i=i: lambda e: e.matmul(
                    out=PS[4][:, 0:E], lhsT=ones_bf[:], rhs=b_rt_bf[:],
                    start=False, stop=True))(), inc=PE)
                S.emit("scalar", lambda e: e.copy(out=RL, in_=PS[4][:, 0:E]),
                       waits=[(PE, S.v[PE])], inc=ACT)
                ps4_ready = S.v[ACT]
                c0, c1 = small[:, 4:5], small[:, 5:6]
                S.emit("vector", lambda e: e.reduce_max(c0, RL, axis=AX.X),
                       waits=[(ACT, S.v[ACT])], inc=DVE)
                S.emit("vector", lambda e: e.tensor_scalar_mul(c0, c0, -1.0), inc=DVE)
                S.emit("scalar", lambda e: e.activation(out=RW[0], in_=RL,
                                                        func=AF.Exp, bias=c0),
                       waits=[(DVE, S.v[DVE])], inc=ACT)
                S.emit("vector", lambda e: e.reduce_sum(c1, RW[0], axis=AX.X),
                       waits=[(ACT, S.v[ACT])], inc=DVE)
                S.emit("vector", lambda e: e.reciprocal(c1, c1), inc=DVE)
                S.emit("vector", lambda e: e.tensor_scalar_mul(RW[1], RW[0], c1),
                       inc=DVE)
                S.emit("vector", lambda e: e.reduce_max(c0, RW[1], axis=AX.X), inc=DVE)
                S.emit("vector", lambda e: e.tensor_single_scalar(
                    RW[2], RW[1], c0, ALU.is_equal), inc=DVE)
                S.emit("vector", lambda e: e.scalar_tensor_tensor(
                    out=RW[2], in0=RW[2], scalar=-1e30, in1=RW[1], op0=ALU.mult,
                    op1=ALU.add), inc=DVE)
                S.emit("vector", lambda e: e.reduce_max(c0, RW[2], axis=AX.X), inc=DVE)
                S.emit("vector", lambda e: e.tensor_single_scalar(
                    RW[3], RW[1], c0, ALU.is_ge), inc=DVE)
                S.emit("vector", (lambda i=i: lambda e: e.scalar_tensor_tensor(
                    out=w_all[:, i * E:(i + 1) * E], in0=RW[1], scalar=0.5,
                    in1=RW[3], op0=ALU.mult, op1=ALU.mult))(), inc=DVE)
            for i in range(NT):
                S.emit("vector", (lambda i=i: lambda e: e.tensor_copy(
                    vbf(AR1, 13872, E), w_all[:, i * E:(i + 1) * E]))(),
                    waits=[(ACT, S.v[ACT])], inc=DVE)
                S.emit("tensor", (lambda i=i: lambda e: e.transpose(
                    out=PBF[0:E, 0:P], in_=vbf(AR1, 13872, E),
                    identity=idm[:]))(), waits=[(DVE, S.v[DVE])], inc=PE)
                S.emit("scalar", (lambda i=i: lambda e: e.copy(
                    out=wT8_bf[:, i * P:(i + 1) * P], in_=PBF[0:E, 0:P]))(),
                    waits=[(PE, S.v[PE])], inc=ACT)
            bias_ready = bexv
            wT8_done = S.v[ACT]
            for i in range(NT):
                for nt in range(2):
                    S.emit("tensor", (lambda i=i, nt=nt: lambda e: e.matmul(
                        out=PS[4 + nt][:], lhsT=wT8_bf[:, i * P:(i + 1) * P],
                        rhs=b_ex_bf[:, nt * 512:(nt + 1) * 512],
                        start=True, stop=True))(),
                        waits=[(ACT, wT8_done), (DVE, bias_ready)], inc=PE)
                pev = S.v[PE]
                for nt in range(2):
                    S.emit("vector", (lambda i=i, nt=nt: lambda e: e.tensor_add(
                        xtile(i)[:, nt * 512:(nt + 1) * 512], PS[4 + nt][:],
                        xtile(i)[:, nt * 512:(nt + 1) * 512]))(),
                        waits=[(PE, pev)], inc=DVE)
                bias_ready = S.v[DVE]
            bank_ready = [0, 0, 0, 0]
            wt_free = [0, 0]
            stg_free = [0, 0]
            stgb_free = [0, 0]
            for e_i in range(E):
                slot = e_i % 2
                if call == 0:
                    for ot in range(KC):
                        stg = WSTG[ot % 2]
                        stgb = WSTG_BF[ot % 2]
                        S.emit("sync", (lambda e_i=e_i, ot=ot, stg=stg:
                                        lambda e: e.dma_start(
                            out=stg, in_=W_ex_d[e_i, ot * 128:(ot + 1) * 128, :]))(),
                            waits=[(DVE, stg_free[ot % 2])], inc=LD, inc_n=16)
                        S.emit("vector", (lambda stg=stg, stgb=stgb:
                                          lambda e: e.tensor_copy(stgb, stg))(),
                            waits=[(LD, S.v[LD]), (PE, stgb_free[ot % 2])],
                            inc=DVE)
                        stg_free[ot % 2] = S.v[DVE]
                        dvv = S.v[DVE]
                        for k in range(KC):
                            S.emit("tensor", (lambda k=k, stgb=stgb:
                                              lambda e: e.transpose(
                                out=PBF[:, k * 128:(k + 1) * 128],
                                in_=stgb[:, k * 128:(k + 1) * 128],
                                identity=idm[:]))(),
                                waits=[(DVE, dvv), (ACT, S.v[ACT])], inc=PE)
                        stgb_free[ot % 2] = S.v[PE]
                        dst = WTv(slot).rearrange("p (k o) -> p k o", k=KC)[
                            :, :, ot * 128:(ot + 1) * 128]
                        S.emit("scalar", (lambda dst=dst: lambda e: e.copy(
                            out=dst,
                            in_=PBF[:].rearrange("p (k b) -> p k b", k=KC)))(),
                            waits=[(PE, S.v[PE])], inc=ACT)
                    S.emit("gpsimd", (lambda e_i=e_i, slot=slot:
                                      lambda e: e.dma_start(
                        out=wT_scr[e_i], in_=WTv(slot)))(),
                        waits=[(ACT, S.v[ACT])], inc=GD, inc_n=16)
                    wt_ok = [(ACT, S.v[ACT])]
                else:
                    S.emit("sync", (lambda e_i=e_i, slot=slot:
                                    lambda e: e.dma_start(
                        out=WTv(slot), in_=wT_scr[e_i]))(),
                        waits=[(PE, wt_free[slot])], inc=LD, inc_n=16)
                    wt_ok = [(LD, S.v[LD])]
                for i in range(NT):
                    for nt in range(2):
                        qi = slot * 2 + nt
                        q = PS[qi]
                        for k in range(KC):
                            S.emit("tensor", (lambda i=i, nt=nt, k=k, q=q, slot=slot:
                                              lambda e: e.matmul(
                                out=q[:],
                                lhsT=XT[:, k * T + i * P: k * T + (i + 1) * P],
                                rhs=WTv(slot).rearrange("p (k o) -> p k o", k=KC)[
                                    :, k, nt * 512:(nt + 1) * 512],
                                start=(k == 0), stop=(k == KC - 1)))(),
                                waits=wt_ok + [(DVE, bank_ready[qi])], inc=PE)
                        pev = S.v[PE]
                        S.emit("vector", (lambda i=i, nt=nt, q=q, e_i=e_i:
                                          lambda e: e.scalar_tensor_tensor(
                            out=xtile(i)[:, nt * 512:(nt + 1) * 512], in0=q[:],
                            scalar=w_all[:, i * E + e_i:i * E + e_i + 1],
                            in1=xtile(i)[:, nt * 512:(nt + 1) * 512],
                            op0=ALU.mult, op1=ALU.add))(),
                            waits=[(PE, pev)], inc=DVE)
                        bank_ready[qi] = S.v[DVE]
                wt_free[slot] = S.v[PE]

        emit_ln_all(evac_to_xt=True)
        if upto == "ln1":
            S.emit("vector", lambda e: e.tensor_copy(v32(AR1, 0, 2048),
                                                     XT[:, 0:2048]),
                   waits=[(ACT, S.v[ACT])], inc=DVE)
            S.emit("gpsimd", lambda e: e.dma_start(
                out=x_scr[0:2].rearrange("a p d -> p a d"),
                in_=v32(AR1, 0, 2048)), waits=[(DVE, S.v[DVE])],
                inc=GD, inc_n=16)
            for i in range(NT):
                S.emit("gpsimd", (lambda i=i: lambda e: e.dma_start(
                    out=out_d[i * P:(i + 1) * P, :], in_=xtile(i)))(),
                    waits=[(DVE, S.v[DVE]), (ACT, S.v[ACT])], inc=GD, inc_n=16)
            S.emit("gpsimd", lambda e: e.nop(), waits=[(GD, S.v[GD])])
            S.finalize(block)
            return nc
        emit_ffn(0)
        emit_ln_all(evac_to_xt=False)
        for i in range(NT):
            S.emit("gpsimd", (lambda i=i: lambda e: e.dma_start(
                out=x_scr[i], in_=xtile(i)))(),
                waits=[(DVE, S.v[DVE])], inc=GD, inc_n=16)

        if upto == "ffn1":
            for i in range(NT):
                S.emit("gpsimd", (lambda i=i: lambda e: e.dma_start(
                    out=out_d[i * P:(i + 1) * P, :], in_=xtile(i)))(),
                    waits=[(DVE, S.v[DVE]), (ACT, S.v[ACT])], inc=GD, inc_n=16)
            if debug_scr:
                S.emit("gpsimd", lambda e: e.dma_start(
                    out=x_scr[0, :, 0:NT * E], in_=w_all[:, :]),
                    waits=[(DVE, S.v[DVE])], inc=GD, inc_n=16)
                S.emit("vector", lambda e: e.tensor_copy(SQ, XT[:, 0:1024]),
                       inc=DVE)
                S.emit("gpsimd", lambda e: e.dma_start(
                    out=x_scr[1], in_=SQ), waits=[(DVE, S.v[DVE])],
                    inc=GD, inc_n=16)
            S.emit("gpsimd", lambda e: e.nop(), waits=[(GD, S.v[GD])])
            S.finalize(block)
            return nc

        brep_free = 0
        ugd_last = 0
        dtu_dma_prev = 0
        for g2 in range(G):
            S.emit("vector", (lambda g2=g2: lambda e: e.memset(
                xinv(g2, 4, off=0), 0.0))(),
                waits=[(GD, S.v[GD]), (ACT, S.v[ACT])], inc=DVE)
        for c in range(NCH):
            for h in range(H):
                S.emit("sync", (lambda c=c, h=h: lambda e: e.dma_start(
                    out=XT2C[c % 2].rearrange("p (h t) -> p h t", h=H)[:, h, :],
                    in_=xT2_scr[h, :, c * C:(c + 1) * C]))(),
                    waits=[(PE, S.v[PE]), (GD, S.v[GD])], inc=LD, inc_n=16)
            for s_ in range(2):
                S.emit("gpsimd", (lambda c=c, s_=s_: lambda e: e.dma_start(
                    out=XRES[s_], in_=x_scr[2 * c + s_]))(),
                    waits=[(DVE, S.v[DVE])], inc=GD, inc_n=16)
            ld_xt2 = S.v[LD]
            for h in range(H):
                pz_a, pz_b = PS[(h % 2) * 2], PS[(h % 2) * 2 + 1]
                rhs = XT2C[c % 2].rearrange("p (hh t) -> p hh t", hh=H)[:, h, :]
                for mi in range(4):
                    q = pz_a if mi < 2 else pz_b
                    col = (mi % 2) * 256
                    S.emit("tensor", (lambda h=h, mi=mi, q=q, col=col, rhs=rhs:
                                      lambda e: e.matmul(
                        out=q[:, col:col + 256],
                        lhsT=W_in_bf[:, h * 512 + mi * 128: h * 512 + (mi + 1) * 128],
                        rhs=rhs, start=True, stop=True))(),
                        waits=[(LD, ld_xt2), (ACT, S.v[ACT]), (DVE, S.v[DVE])],
                        inc=PE)
                pev = S.v[PE]
                g2 = 2 * h
                xin_pair = AR2[:, g2 * (C + 4):(g2 + 2) * (C + 4)].rearrange(
                    "p (two cc) -> p two cc", two=2)[:, :, 4:4 + C]
                S.emit("scalar", (lambda xin_pair=xin_pair, pz_a=pz_a:
                                  lambda e: e.copy(
                    out=xin_pair,
                    in_=pz_a[:].rearrange("p (two cc) -> p two cc", two=2)))(),
                    waits=[(PE, pev), (DVE, S.v[DVE])], inc=ACT)
                S.emit("scalar", (lambda g2=g2, pz_b=pz_b: lambda e: e.activation(
                    out=v32(AR2, SZOFF + g2 * C, 2 * C), in_=pz_b[:],
                    func=AF.Silu))(), inc=ACT)
                actv = S.v[ACT]
                for dh in range(2):
                    g2 = 2 * h + dh
                    acc = CONVACC[dh]
                    S.emit("vector", (lambda g2=g2, acc=acc: lambda e: e.tensor_scalar(
                        out=acc, in0=xinv(g2, C, off=1),
                        scalar1=conv_w_s[:, g2 * 4:g2 * 4 + 1],
                        scalar2=conv_b_s[:, g2:g2 + 1],
                        op0=ALU.mult, op1=ALU.add))(),
                        waits=[(ACT, actv)], inc=DVE)
                    for j in range(1, 4):
                        S.emit("vector", (lambda g2=g2, j=j, acc=acc:
                                          lambda e: e.scalar_tensor_tensor(
                            out=acc, in0=xinv(g2, C, off=1 + j),
                            scalar=conv_w_s[:, g2 * 4 + j:g2 * 4 + j + 1],
                            in1=acc, op0=ALU.mult, op1=ALU.add))(), inc=DVE)
                    S.emit("vector", (lambda g2=g2: lambda e: e.tensor_copy(
                        xinv(g2, 4, off=0), xinv(g2, 4, off=C)))(), inc=DVE)
                    dvv = S.v[DVE]
                    S.emit("scalar", (lambda g2=g2, acc=acc: lambda e: e.activation(
                        out=xinv(g2, C, off=4), in_=acc, func=AF.Silu))(),
                        waits=[(DVE, dvv)], inc=ACT)
                    S.emit("vector", (lambda g2=g2: lambda e: e.tensor_copy(
                        xcbfv(g2), xinv(g2, C, off=4)))(),
                        waits=[(ACT, S.v[ACT])], inc=DVE)
            for h in range(H):
                xc0, xc1 = xcbfv(2 * h), xcbfv(2 * h + 1)
                mm = [(PS[4], 0, 0, DTR), (PS[4], 256, DTR, DTR + NS),
                      (PS[5], 0, DTR + NS, DTR + 2 * NS)]
                for (q, col, m0, m1) in mm:
                    for k in range(2):
                        rhs = xc0 if k == 0 else xc1
                        S.emit("tensor", (lambda q=q, col=col, m0=m0, m1=m1, k=k,
                                          rhs=rhs, h=h: lambda e: e.matmul(
                            out=q[0:m1 - m0, col:col + 256],
                            lhsT=W_xp_bf[:, h * 80 + k * 40 + m0:
                                         h * 80 + k * 40 + m1],
                            rhs=rhs, start=(k == 0), stop=(k == 1)))(),
                            waits=[(DVE, S.v[DVE]), (ACT, S.v[ACT])], inc=PE)
                pev = S.v[PE]
                S.emit("scalar", lambda e: e.copy(out=dtr_bf[:],
                                                  in_=PS[4][0:DTR, 0:256]),
                       waits=[(PE, pev), (GD, S.v[GD])], inc=ACT)
                S.emit("scalar", lambda e: e.copy(out=bcs[0:NS, 0:C],
                                                  in_=PS[4][0:NS, 256:512]),
                       inc=ACT)
                S.emit("scalar", lambda e: e.copy(out=bcs[0:NS, C:2 * C],
                                                  in_=PS[5][0:NS, 0:256]), inc=ACT)
                actv = S.v[ACT]
                S.emit("gpsimd", (lambda h=h, c=c: lambda e: e.dma_start(
                    out=bc_scr[h, 0, :, c * C:(c + 1) * C], in_=bcs[0:NS, 0:C]))(),
                    waits=[(ACT, actv)], inc="s_bc", inc_n=16)
                S.emit("gpsimd", (lambda h=h, c=c: lambda e: e.dma_start(
                    out=bc_scr[h, 1, :, c * C:(c + 1) * C],
                    in_=bcs[0:NS, C:2 * C]))(), waits=[(ACT, actv)],
                    inc="s_bc", inc_n=16)
                brep3 = BREP.rearrange("p (n t) -> p n t", n=NS)
                crep3 = CREP.rearrange("p (n t) -> p n t", n=NS)
                srcB = bc_scr[h, 0, :, c * C:(c + 1) * C].unsqueeze(0)\
                    .broadcast_to([NS, NS, C])
                srcC = bc_scr[h, 1, :, c * C:(c + 1) * C].unsqueeze(0)\
                    .broadcast_to([NS, NS, C])
                S.emit("sync", (lambda h=h, srcB=srcB, brep3=brep3:
                                lambda e: e.dma_start(
                    out=brep3[h * NS:(h + 1) * NS], in_=srcB))(),
                    waits=[("s_bc", S.v["s_bc"]), (DVE, brep_free)],
                    inc=LD, inc_n=16)
                S.emit("sync", (lambda h=h, srcC=srcC, crep3=crep3:
                                lambda e: e.dma_start(
                    out=crep3[h * NS:(h + 1) * NS], in_=srcC))(),
                    waits=[("s_bc", S.v["s_bc"]), (DVE, brep_free)],
                    inc=LD, inc_n=16)
                for m in range(2):
                    S.emit("tensor", (lambda h=h, m=m: lambda e: e.matmul(
                        out=PS[3][:, m * 256:(m + 1) * 256],
                        lhsT=W_dt_bf[:, h * DI + m * 128: h * DI + (m + 1) * 128],
                        rhs=dtr_bf[:], start=True, stop=True))(),
                        waits=[(ACT, actv), (DVE, S.v[DVE])], inc=PE)
                pev = S.v[PE]
                g2 = 2 * h
                X2 = v32(AR2, 14400, 2 * C)
                E2 = v32(AR2, 15168, 2 * C)
                dt_pair = v32(AR2, DTOFF + g2 * C, 2 * C)
                S.emit("scalar", lambda e: e.copy(out=X2, in_=PS[3][:]),
                       waits=[(PE, pev), (DVE, S.v[DVE])], inc=ACT)
                for dh in range(2):
                    S.emit("vector", (lambda g2=g2, dh=dh: lambda e:
                                      e.tensor_scalar_add(
                        v32(AR2, 14400 + dh * C, C), v32(AR2, 14400 + dh * C, C),
                        b_dt_s[:, g2 + dh:g2 + dh + 1]))(),
                        waits=[(ACT, S.v[ACT])], inc=DVE)
                S.emit("vector", lambda e: e.scalar_tensor_tensor(
                    out=E2, in0=X2, scalar=-1.0, in1=X2,
                    op0=ALU.mult, op1=ALU.max), inc=DVE)
                S.emit("scalar", lambda e: e.activation(
                    out=E2, in_=E2, func=AF.Exp, scale=-1.0),
                    waits=[(DVE, S.v[DVE])], inc=ACT)
                S.emit("scalar", lambda e: e.activation(
                    out=E2, in_=E2, func=AF.Ln, bias=small[:, 7:8]), inc=ACT)
                S.emit("vector", lambda e: e.tensor_scalar_max(X2, X2, 0.0),
                       waits=[(ACT, S.v[ACT])], inc=DVE)
                S.emit("vector", (lambda dt_pair=dt_pair, X2=X2, E2=E2:
                                  lambda e: e.tensor_add(dt_pair, X2, E2))(),
                       waits=[(GD, dtu_dma_prev)], inc=DVE)
                xc_pair = AR2[:, g2 * (C + 4):(g2 + 2) * (C + 4)].rearrange(
                    "p (two cc) -> p two cc", two=2)[:, :, 4:4 + C]
                S.emit("vector", (lambda dt_pair=dt_pair, xc_pair=xc_pair:
                                  lambda e: e.tensor_mul(
                    v32(AR2, 15680, 2 * C).rearrange("p (two cc) -> p two cc",
                                                     two=2),
                    dt_pair.rearrange("p (two cc) -> p two cc", two=2),
                    xc_pair))(), waits=[(GD, ugd_last)], inc=DVE)
                dvv = S.v[DVE]
                for dh in range(2):
                    g2b = 2 * h + dh
                    S.emit("gpsimd", (lambda g2b=g2b, h=h, dh=dh, c=c:
                                      lambda e: e.dma_start(
                        out=dt_scr[h, dh * 128:(dh + 1) * 128, c * C:(c + 1) * C],
                        in_=dtv(g2b)))(), waits=[(DVE, dvv)], inc=GD, inc_n=16)
                    S.emit("gpsimd", (lambda h=h, dh=dh, c=c:
                                      lambda e: e.dma_start(
                        out=u_scr[h, dh * 128:(dh + 1) * 128, c * C:(c + 1) * C],
                        in_=UBLK[dh]))(), waits=[(DVE, dvv)], inc=GD, inc_n=16)
                ugd_last = S.v[GD]
            dtu_dma_prev = S.v[GD]
            ld_bc = S.v[LD]
            gd_dtu = S.v[GD]
            for g in range(G):
                sl = g % 2
                src_dt = dt_scr[:, g * NS:(g + 1) * NS, c * C:(c + 1) * C]
                src_u = u_scr[:, g * NS:(g + 1) * NS, c * C:(c + 1) * C]
                S.emit("sync", (lambda src_dt=src_dt, sl=sl: lambda e: e.dma_start(
                    out=DTI[sl], in_=src_dt))(),
                    waits=[(GD, gd_dtu), (ACT, S.v[ACT]), (DVE, S.v[DVE])],
                    inc=LD, inc_n=16)
                S.emit("sync", (lambda src_u=src_u, sl=sl: lambda e: e.dma_start(
                    out=UI[sl], in_=src_u))(),
                    waits=[(GD, gd_dtu), (DVE, S.v[DVE])], inc=LD, inc_n=16)
                ldv = S.v[LD]
                da3 = DA[sl].rearrange("p (n t) -> p n t", n=NS)
                S.emit("vector", (lambda g=g, sl=sl, da3=da3: lambda e: e.tensor_tensor(
                    out=da3,
                    in0=DTI[sl].unsqueeze(1).broadcast_to([P, NS, C]),
                    in1=A_neg[:, g * NS:(g + 1) * NS].unsqueeze(2)
                    .broadcast_to([P, NS, C]), op=ALU.mult))(),
                    waits=[(LD, ldv), (ACT, S.v[ACT])], inc=DVE)
                S.emit("scalar", (lambda sl=sl: lambda e: e.activation(
                    out=DA[sl], in_=DA[sl], func=AF.Exp))(),
                    waits=[(DVE, S.v[DVE])], inc=ACT)
                dbx3 = DBX[sl].rearrange("p (n t) -> p n t", n=NS)
                S.emit("vector", (lambda sl=sl, dbx3=dbx3: lambda e: e.tensor_tensor(
                    out=dbx3,
                    in0=UI[sl].unsqueeze(1).broadcast_to([P, NS, C]),
                    in1=BREP.rearrange("p (n t) -> p n t", n=NS),
                    op=ALU.mult))(), waits=[(LD, ld_bc)], inc=DVE)
                da0 = DA[sl].rearrange("p (n t) -> p n t", n=NS)[:, :, 0]
                db0 = DBX[sl].rearrange("p (n t) -> p n t", n=NS)[:, :, 0]
                tmpv = small[:, 16:32]
                S.emit("vector", (lambda g=g, da0=da0, tmpv=tmpv:
                                  lambda e: e.tensor_mul(
                    tmpv, da0, carry[:, g * NS:(g + 1) * NS]))(),
                    waits=[(ACT, S.v[ACT])], inc=DVE)
                S.emit("vector", (lambda da0=da0: lambda e: e.memset(da0, 0.0))(),
                       inc=DVE)
                S.emit("vector", (lambda db0=db0, tmpv=tmpv:
                                  lambda e: e.tensor_add(db0, tmpv, db0))(), inc=DVE)
                S.emit("vector", (lambda sl=sl: lambda e: e.tensor_tensor_scan(
                    out=DBX[sl], data0=DA[sl], data1=DBX[sl], initial=0.0,
                    op0=ALU.mult, op1=ALU.add))(), inc=DVE)
                hv3 = DBX[sl].rearrange("p (n t) -> p n t", n=NS)
                S.emit("vector", (lambda g=g, hv3=hv3: lambda e: e.tensor_copy(
                    carry[:, g * NS:(g + 1) * NS], hv3[:, :, C - 1]))(), inc=DVE)
                S.emit("vector", (lambda sl=sl: lambda e: e.tensor_tensor(
                    out=DA[sl], in0=DBX[sl], in1=CREP, op=ALU.mult))(), inc=DVE)
                width = NS
                while width > 1:
                    half = width // 2
                    a = DA[sl][:, 0:half * C]
                    b = DA[sl][:, half * C:width * C]
                    S.emit("vector", (lambda a=a, b=b: lambda e: e.tensor_add(
                        a, a, b))(), inc=DVE)
                    width = half
                S.emit("vector", (lambda sl=sl: lambda e: e.tensor_copy(
                    YI[sl], DA[sl][:, 0:C]))(), waits=[(GD, S.v[GD])], inc=DVE)
                dvv = S.v[DVE]
                dst_y = y_scr[:, g * NS:(g + 1) * NS, c * C:(c + 1) * C]
                S.emit("gpsimd", (lambda dst_y=dst_y, sl=sl: lambda e: e.dma_start(
                    out=dst_y, in_=YI[sl]))(), waits=[(DVE, dvv)], inc=GD, inc_n=16)
            gd_y = S.v[GD]
            brep_free = S.v[DVE]
            yld = {}
            for g2 in range(G):
                h, dh = g2 // 2, g2 % 2
                S.emit("sync", (lambda h=h, dh=dh, c=c, g2=g2:
                                lambda e: e.dma_start(
                    out=yblk(g2),
                    in_=y_scr[h, dh * 128:(dh + 1) * 128,
                              c * C:(c + 1) * C]))(),
                    waits=[(GD, gd_y), (DVE, S.v[DVE])], inc=LD, inc_n=16)
                yld[g2] = S.v[LD]
            for h in range(H):
                for dh in range(2):
                    g2 = 2 * h + dh
                    S.emit("vector", (lambda g2=g2: lambda e: e.scalar_tensor_tensor(
                        out=yblk(g2), in0=xinv(g2, C, off=4),
                        scalar=Dp_s[:, g2:g2 + 1], in1=yblk(g2),
                        op0=ALU.mult, op1=ALU.add))(),
                        waits=[(LD, yld[g2])], inc=DVE)
                g2 = 2 * h
                S.emit("vector", (lambda g2=g2: lambda e: e.tensor_mul(
                    AR3[:].bitcast(BF16)[:, 4096 + g2 * 256:4096 + (g2 + 2) * 256],
                    v32(AR2, DTOFF + g2 * C, 2 * C),
                    v32(AR2, SZOFF + g2 * C, 2 * C)))(), inc=DVE)
                dvv = S.v[DVE]
                for tsub in range(2):
                    for k in range(2):
                        S.emit("tensor", (lambda h=h, tsub=tsub, k=k:
                                          lambda e: e.matmul(
                            out=PS[5][:, tsub * 128:(tsub + 1) * 128],
                            lhsT=ybfv(2 * h + k)[:, tsub * 128:(tsub + 1) * 128],
                            rhs=W_out_bf[:, (h * 2 + k) * DH:(h * 2 + k + 1) * DH],
                            start=(k == 0), stop=(k == 1)))(),
                            waits=[(DVE, dvv)], inc=PE)
                pev = S.v[PE]
                for tsub in range(2):
                    S.emit("vector", (lambda h=h, tsub=tsub: lambda e: e.tensor_add(
                        XRES[tsub][:, h * 128:(h + 1) * 128],
                        PS[5][:, tsub * 128:(tsub + 1) * 128],
                        XRES[tsub][:, h * 128:(h + 1) * 128]))(),
                        waits=[(PE, pev), (GD, S.v[GD])], inc=DVE)
            for s_ in range(2):
                S.emit("gpsimd", (lambda c=c, s_=s_: lambda e: e.dma_start(
                    out=x_scr[2 * c + s_], in_=XRES[s_]))(),
                    waits=[(DVE, S.v[DVE])], inc=GD, inc_n=16)

        gd_m = S.v[GD]
        for i in range(NT):
            S.emit("sync", (lambda i=i: lambda e: e.dma_start(
                out=xtile(i), in_=x_scr[i]))(),
                waits=[(GD, gd_m), (DVE, S.v[DVE]), (ACT, S.v[ACT]),
                       (PE, S.v[PE])], inc=LD, inc_n=16)
        if upto != "mamba":
            emit_ln_all(evac_to_xt=True)
            emit_ffn(1)
        for i in range(NT):
            S.emit("gpsimd", (lambda i=i: lambda e: e.dma_start(
                out=out_d[i * P:(i + 1) * P, :], in_=xtile(i)))(),
                waits=[(DVE, S.v[DVE]), (ACT, S.v[ACT])], inc=GD, inc_n=16)
        S.emit("gpsimd", lambda e: e.nop(), waits=[(GD, S.v[GD])])

        S.finalize(block)
    return nc


def kernel(L0_attn_x, L0_attn_W_in, L0_attn_conv_w, L0_attn_conv_b, L0_attn_W_xp,
           L0_attn_W_dt, L0_attn_b_dt, L0_attn_A_log, L0_attn_Dp, L0_attn_W_out,
           L0_ffn_W_experts, L0_ffn_b_experts, L0_ffn_W_router, L0_ffn_b_router):
    nc = build()
    idm = np.eye(P, dtype=ml_dtypes.bfloat16)
    base = dict(
        W_in=np.ascontiguousarray(L0_attn_W_in, np.float32),
        conv_w=np.ascontiguousarray(L0_attn_conv_w, np.float32),
        conv_b=np.ascontiguousarray(L0_attn_conv_b, np.float32),
        W_xp=np.ascontiguousarray(L0_attn_W_xp, np.float32),
        W_dt=np.ascontiguousarray(L0_attn_W_dt, np.float32),
        b_dt=np.ascontiguousarray(L0_attn_b_dt, np.float32),
        A_log=np.ascontiguousarray(L0_attn_A_log, np.float32),
        Dp=np.ascontiguousarray(L0_attn_Dp, np.float32),
        W_out=np.ascontiguousarray(L0_attn_W_out, np.float32),
        W_ex=np.ascontiguousarray(L0_ffn_W_experts, np.float32),
        b_ex=np.ascontiguousarray(L0_ffn_b_experts, np.float32),
        W_rt=np.ascontiguousarray(L0_ffn_W_router, np.float32),
        b_rt=np.ascontiguousarray(L0_ffn_b_router, np.float32).reshape(1, E),
        idm=idm,
    )
    x = np.ascontiguousarray(L0_attn_x, np.float32)
    in_maps = [dict(x=x[c], **base) for c in range(8)]
    res = run_bass_kernel_spmd(nc, in_maps, core_ids=list(range(8)))
    return np.stack([np.asarray(res.results[c]["out"]) for c in range(8)])


# version 09
# speedup vs baseline: 1.1823x; 1.0681x over previous
import numpy as np
import ml_dtypes
import contextlib
import concourse.bass as bass
import concourse.mybir as mybir
from concourse.bass_utils import run_bass_kernel_spmd

F32 = mybir.dt.float32
BF16 = mybir.dt.bfloat16
AF = mybir.ActivationFunctionType
ALU = mybir.AluOpType
AX = mybir.AxisListType

P = 128
T = 2048
D = 1024
NT = 16
E = 8
KC = 8
H = 8
DH = 128
DI = 256
NS = 16
DTR = 8
C = 256
NCH = T // C
G = 16

ENGINES = ("sync", "gpsimd", "vector", "scalar", "tensor")


class Sched:

    def __init__(self, sems, paranoid=False):
        self.ops = {e: [] for e in ENGINES}
        self.sems = sems
        self.v = {n: 0 for n in sems}
        self.paranoid = paranoid

    SELF = {"vector": "s_dve", "scalar": "s_act"}

    def emit(self, eng, fn, waits=(), inc=None, inc_n=1, drain=False):
        if self.paranoid:
            waits = [(n, v) for n, v in self.v.items()]
        else:
            own = self.SELF.get(eng)
            if own is not None:
                waits = list(waits) + [(own, self.v[own])]
        w = [(self.sems[s], v) for (s, v) in waits if v > 0]
        self.ops[eng].append((w, fn, (self.sems[inc], inc_n) if inc else None,
                              drain))
        if inc:
            self.v[inc] += inc_n

    def barrier_vals(self):
        return dict(self.v)

    def wait_all(self):
        return [(n, self.v[n]) for n in self.sems]

    def finalize(self, block):
        for eng in ENGINES:
            ops = self.ops[eng]
            if not ops:
                continue

            def make(eng_ops):
                def body(e):
                    seen = {}
                    for waits, fn, inc, drain in eng_ops:
                        for s, v in waits:
                            if seen.get(s.name, -1) < v:
                                e.wait_ge(s, v)
                                seen[s.name] = v
                        if drain:
                            e.drain()
                        ins = fn(e)
                        if inc is not None:
                            ins.then_inc(inc[0], inc[1])
                return body
            getattr(block, eng)(make(ops))


def build(debug_tap=None, debug_scr=False, upto="full", paranoid=False):
    nc = bass.Bass("TRN2", target_bir_lowering=False)
    scr_kind = "ExternalOutput" if debug_scr else "Internal"

    x_d = nc.dram_tensor("x", [T, D], F32, kind="ExternalInput").ap()
    W_in_d = nc.dram_tensor("W_in", [H, DH, 2 * DI], F32, kind="ExternalInput").ap()
    conv_w_d = nc.dram_tensor("conv_w", [H, DI, 4], F32, kind="ExternalInput").ap()
    conv_b_d = nc.dram_tensor("conv_b", [H, DI], F32, kind="ExternalInput").ap()
    W_xp_d = nc.dram_tensor("W_xp", [H, DI, DTR + 2 * NS], F32, kind="ExternalInput").ap()
    W_dt_d = nc.dram_tensor("W_dt", [H, DTR, DI], F32, kind="ExternalInput").ap()
    b_dt_d = nc.dram_tensor("b_dt", [H, DI], F32, kind="ExternalInput").ap()
    A_log_d = nc.dram_tensor("A_log", [H, DI, NS], F32, kind="ExternalInput").ap()
    Dp_d = nc.dram_tensor("Dp", [H, DI], F32, kind="ExternalInput").ap()
    W_out_d = nc.dram_tensor("W_out", [H, DI, DH], F32, kind="ExternalInput").ap()
    W_ex_d = nc.dram_tensor("W_ex", [E, D, D], F32, kind="ExternalInput").ap()
    b_ex_d = nc.dram_tensor("b_ex", [E, D], F32, kind="ExternalInput").ap()
    W_rt_d = nc.dram_tensor("W_rt", [E, D], F32, kind="ExternalInput").ap()
    b_rt_d = nc.dram_tensor("b_rt", [1, E], F32, kind="ExternalInput").ap()
    idm_d = nc.dram_tensor("idm", [P, P], BF16, kind="ExternalInput").ap()

    out_d = nc.dram_tensor("out", [T, D], F32, kind="ExternalOutput").ap()
    tap_d = None
    if debug_tap is not None:
        tap_d = nc.dram_tensor("tap", list(debug_tap), F32, kind="ExternalOutput").ap()

    x_scr = nc.dram_tensor("x_scr", [NT, P, D], F32, kind=scr_kind).ap()
    xT2_scr = nc.dram_tensor("xT2_scr", [KC, P, T], BF16, kind="Internal").ap()
    wT_scr = nc.dram_tensor("wT_scr", [E, P, KC * D], BF16, kind="Internal").ap()
    bc_scr = nc.dram_tensor("bc_scr", [H, 2, NS, T], F32, kind=scr_kind).ap()
    dt_scr = nc.dram_tensor("dt_scr", [H, DI, T], F32, kind=scr_kind).ap()
    u_scr = nc.dram_tensor("u_scr", [H, DI, T], F32, kind=scr_kind).ap()
    y_scr = nc.dram_tensor("y_scr", [H, DI, T], F32, kind=scr_kind).ap()

    with contextlib.ExitStack() as st:
        def sb(name, pdim, cols_f32):
            return st.enter_context(nc.sbuf_tensor(name, [pdim, cols_f32], F32))

        AR1 = sb("AR1", P, 20480)
        AR2 = sb("AR2", P, 16384)
        AR3 = sb("AR3", P, 8192)

        W_in_bf = st.enter_context(nc.sbuf_tensor("W_in_bf", [P, H * 512], BF16))
        W_xp_bf = st.enter_context(nc.sbuf_tensor("W_xp_bf", [P, H * 2 * 40], BF16))
        W_dt_bf = st.enter_context(nc.sbuf_tensor("W_dt_bf", [DTR, H * DI], BF16))
        W_out_bf = st.enter_context(nc.sbuf_tensor("W_out_bf", [P, H * 2 * DH], BF16))
        W_rtT_bf = st.enter_context(nc.sbuf_tensor("W_rtT_bf", [P, KC * E], BF16))
        b_rt_bf = st.enter_context(nc.sbuf_tensor("b_rt_bf", [1, E], BF16))
        idm = st.enter_context(nc.sbuf_tensor("idm_s", [P, P], BF16))
        ones_bf = st.enter_context(nc.sbuf_tensor("ones_bf", [1, P], BF16))
        A_neg = sb("A_neg", P, G * NS)
        conv_w_s = sb("conv_w_s", P, G * 4)
        conv_b_s = sb("conv_b_s", P, G)
        Dp_s = sb("Dp_s", P, G)
        b_dt_s = sb("b_dt_s", P, G)
        w_all = sb("w_all", P, NT * E)
        carry = sb("carry", P, G * NS)
        small = sb("small", P, 64)
        dtr_bf = st.enter_context(nc.sbuf_tensor("dtr_bf", [DTR, C], BF16))
        bcs = sb("bcs", P, 2 * C)

        PS = [st.enter_context(nc.psum_tensor(f"PS{i}", [P, 512], F32))
              for i in range(6)]
        PBF = st.enter_context(nc.psum_tensor("PBF", [P, 1024], BF16))

        semnames = ["s_ld", "s_gd", "s_pe", "s_act", "s_dve", "s_bc", "s_dti0", "s_dti1", "s_br", "s_ws0", "s_ws1", "s_ws2", "s_ws3", "s_yl", "s_yd0", "s_yd1", "s_xr"] + ["s_bch%d" % i for i in range(8)]
        sems = {n: st.enter_context(nc.semaphore(n)) for n in semnames}
        block = st.enter_context(nc.Block())
        S = Sched(sems, paranoid=(paranoid is True))
        def set_para(phase):
            if isinstance(paranoid, (set, frozenset, list, tuple)):
                S.paranoid = phase in paranoid

        def v32(ar, off, cols, pdim=P):
            return ar[0:pdim, off:off + cols]

        def vbf(ar, off_f32, cols_bf, pdim=P):
            return ar[0:pdim].bitcast(BF16)[0:pdim, 2 * off_f32:2 * off_f32 + cols_bf]

        b_ex_bf = vbf(AR1, 13888, D, pdim=E)
        wT8_bf = vbf(AR1, 14400, T, pdim=E)

        def xtile(i):
            return v32(AR2, i * D, D)

        def WTv(slot):
            return vbf(AR1, slot * 4096, 8192)
        WSTG = [v32(AR1, 8192 + i * 1024, 1024) for i in range(4)]
        WSTG_BF = [vbf(AR1, 12288 + i * 512, 1024) for i in range(2)]
        XNORM = [vbf(AR1, 13312 + i * 512, 1024) for i in range(2)]
        PACK = vbf(AR1, 12288, 1024)
        SQ = v32(AR1, 12800, 1024)
        RL = v32(AR1, 13824, 8)
        RW = [v32(AR1, 13832 + i * 8, 8) for i in range(4)]

        XT = AR3[:].bitcast(BF16)

        DA = [v32(AR1, i * 4096, 4096) for i in range(2)]
        DBX = [v32(AR1, 8192, 4096)] * 2
        BREP = v32(AR1, 12288, 4096)
        CREP = v32(AR1, 16384, 4096)
        def xinv(g2, cols=C + 4, off=0):
            return v32(AR2, g2 * (C + 4) + off, cols)
        SZOFF = G * (C + 4)
        def szv(g2):
            return v32(AR2, SZOFF + g2 * C, C)
        DTOFF = SZOFF + G * C
        def dtv(g2):
            return v32(AR2, DTOFF + g2 * C, C)
        yblk = dtv
        XRES = [v32(AR2, 12352 + i * D, D) for i in range(2)]
        SPX = [v32(AR2, 14400 + i * C, C) for i in range(3)]
        CONVACC = [v32(AR2, 15168 + i * C, C) for i in range(2)]
        UBLK = [v32(AR2, 15680 + i * C, C) for i in range(2)]
        def xcbfv(g2):
            return vbf(AR3, g2 * 128, 256)
        def ybfv(g2):
            return vbf(AR3, 2048 + g2 * 128, 256)
        DTI = [v32(AR3, 4096 + i * C, C) for i in range(2)]
        UI = [v32(AR3, 4608 + i * C, C) for i in range(2)]
        YI = [v32(AR3, 5120 + i * C, C) for i in range(2)]
        XT2C = [vbf(AR3, 5632 + i * 1024, 2048) for i in range(2)]

        LD, GD, PE, ACT, DVE = "s_ld", "s_gd", "s_pe", "s_act", "s_dve"

        set_para("p0")
        for i in range(NT):
            S.emit("sync", (lambda i=i: lambda e: e.dma_start(
                out=xtile(i), in_=x_d[i * P:(i + 1) * P, :]))(), inc=LD, inc_n=16)
        S.emit("sync", lambda e: e.dma_start(out=idm[:], in_=idm_d), inc=LD, inc_n=16)
        for h in range(H):
            S.emit("sync", (lambda h=h: lambda e: e.dma_start(
                out=v32(AR1, 0, 512), in_=W_in_d[h]))(), inc=LD, inc_n=16)
            S.emit("vector", (lambda h=h: lambda e: e.tensor_copy(
                W_in_bf[:, h * 512:(h + 1) * 512], v32(AR1, 0, 512)))(),
                waits=[(LD, S.v[LD])], inc=DVE)
            for k in range(2):
                S.emit("sync", (lambda h=h, k=k: lambda e: e.dma_start(
                    out=v32(AR1, 1024, 40), in_=W_xp_d[h, k * 128:(k + 1) * 128, :]))(),
                    waits=[(DVE, S.v[DVE])], inc=LD, inc_n=16)
                S.emit("vector", (lambda h=h, k=k: lambda e: e.tensor_copy(
                    W_xp_bf[:, (h * 2 + k) * 40:(h * 2 + k + 1) * 40],
                    v32(AR1, 1024, 40)))(), waits=[(LD, S.v[LD])], inc=DVE)
            S.emit("sync", (lambda h=h: lambda e: e.dma_start(
                out=v32(AR1, 2048, DI, pdim=DTR), in_=W_dt_d[h]))(),
                waits=[(DVE, S.v[DVE])], inc=LD, inc_n=16)
            S.emit("vector", (lambda h=h: lambda e: e.tensor_copy(
                W_dt_bf[:, h * DI:(h + 1) * DI], v32(AR1, 2048, DI, pdim=DTR)))(),
                waits=[(LD, S.v[LD])], inc=DVE)
            for k in range(2):
                S.emit("sync", (lambda h=h, k=k: lambda e: e.dma_start(
                    out=v32(AR1, 3072, DH), in_=W_out_d[h, k * 128:(k + 1) * 128, :]))(),
                    waits=[(DVE, S.v[DVE])], inc=LD, inc_n=16)
                S.emit("vector", (lambda h=h, k=k: lambda e: e.tensor_copy(
                    W_out_bf[:, (h * 2 + k) * DH:(h * 2 + k + 1) * DH],
                    v32(AR1, 3072, DH)))(), waits=[(LD, S.v[LD])], inc=DVE)
        S.emit("sync", lambda e: e.dma_start(out=v32(AR1, 5120, E, pdim=1), in_=b_rt_d),
               inc=LD, inc_n=16)
        S.emit("vector", lambda e: e.tensor_copy(b_rt_bf[:], v32(AR1, 5120, E, pdim=1)),
               waits=[(LD, S.v[LD])], inc=DVE)
        S.emit("sync", lambda e: e.dma_start(out=v32(AR1, 6144, D, pdim=E), in_=W_rt_d),
               inc=LD, inc_n=16)
        S.emit("vector", lambda e: e.tensor_copy(
            vbf(AR1, 7168, D, pdim=E), v32(AR1, 6144, D, pdim=E)),
            waits=[(LD, S.v[LD])], inc=DVE)
        for k in range(KC):
            S.emit("tensor", (lambda k=k: lambda e: e.transpose(
                out=PBF[:, 0:E], in_=vbf(AR1, 7168, D, pdim=E)[:, k * 128:(k + 1) * 128],
                identity=idm[0:E, 0:E]))(),
                waits=[(DVE, S.v[DVE]), (ACT, S.v[ACT])], inc=PE)
            S.emit("scalar", (lambda k=k: lambda e: e.copy(
                out=W_rtT_bf[:, k * E:(k + 1) * E], in_=PBF[:, 0:E]))(),
                waits=[(PE, S.v[PE])], inc=ACT)
        for g in range(G):
            h, dh = g // 2, g % 2
            S.emit("sync", (lambda g=g, h=h, dh=dh: lambda e: e.dma_start(
                out=conv_w_s[:, g * 4:(g + 1) * 4],
                in_=conv_w_d[h, dh * 128:(dh + 1) * 128, :]))(), inc=LD, inc_n=16)
            S.emit("sync", (lambda g=g, h=h, dh=dh: lambda e: e.dma_start(
                out=conv_b_s[:, g:g + 1], in_=conv_b_d[h, dh * 128:(dh + 1) * 128]
                .unsqueeze(1), allow_slow_non_contiguous=True))(), inc=LD, inc_n=16)
            S.emit("sync", (lambda g=g, h=h, dh=dh: lambda e: e.dma_start(
                out=Dp_s[:, g:g + 1], in_=Dp_d[h, dh * 128:(dh + 1) * 128]
                .unsqueeze(1), allow_slow_non_contiguous=True))(), inc=LD, inc_n=16)
            S.emit("sync", (lambda g=g, h=h, dh=dh: lambda e: e.dma_start(
                out=b_dt_s[:, g:g + 1], in_=b_dt_d[h, dh * 128:(dh + 1) * 128]
                .unsqueeze(1), allow_slow_non_contiguous=True))(), inc=LD, inc_n=16)
            src = A_log_d[:, g * NS:(g + 1) * NS, :]
            S.emit("sync", (lambda g=g, src=src: lambda e: e.dma_start(
                out=v32(AR1, 8192 + g * NS, NS), in_=src))(), inc=LD, inc_n=16)
        S.emit("scalar", lambda e: e.activation(
            out=v32(AR1, 9216, G * NS), in_=v32(AR1, 8192, G * NS), func=AF.Exp),
            waits=[(LD, S.v[LD])], inc=ACT)
        S.emit("vector", lambda e: e.tensor_scalar_mul(
            A_neg[:], v32(AR1, 9216, G * NS), -1.0),
            waits=[(ACT, S.v[ACT])], inc=DVE)
        S.emit("vector", lambda e: e.memset(ones_bf[:], 1.0), inc=DVE)
        S.emit("vector", lambda e: e.memset(carry[:], 0.0), inc=DVE)
        S.emit("vector", lambda e: e.memset(small[:, 6:7], 1e-5), inc=DVE)
        S.emit("vector", lambda e: e.memset(small[:, 7:8], 1.0), inc=DVE)

        SQ2 = v32(AR1, 15424, 1024)
        MS = small[:, 32:48]
        QS = small[:, 48:64]
        M2S = v32(AR1, 12800, 16)

        def emit_ln_all(evac_to_xt=True):
            for i in range(NT):
                xi = xtile(i)
                sq = SQ if i % 2 == 0 else SQ2
                S.emit("vector", (lambda i=i, xi=xi: lambda e: e.reduce_sum(
                    MS[:, i:i + 1], xi, axis=AX.X))(),
                    waits=[(ACT, S.v[ACT]), (LD, S.v[LD]), (GD, S.v[GD])],
                    inc=DVE)
                S.emit("scalar", (lambda xi=xi, sq=sq: lambda e: e.activation(
                    out=sq, in_=xi, func=AF.Square))(),
                    waits=[(DVE, S.v[DVE])], inc=ACT)
                S.emit("vector", (lambda i=i, sq=sq: lambda e: e.reduce_sum(
                    QS[:, i:i + 1], sq, axis=AX.X))(),
                    waits=[(ACT, S.v[ACT])], inc=DVE)
            S.emit("vector", lambda e: e.tensor_scalar_mul(MS, MS, 1.0 / D),
                   inc=DVE)
            S.emit("vector", lambda e: e.tensor_tensor(out=M2S, in0=MS, in1=MS,
                                                       op=ALU.mult), inc=DVE)
            S.emit("vector", lambda e: e.scalar_tensor_tensor(
                out=QS, in0=QS, scalar=1.0 / D, in1=M2S, op0=ALU.mult,
                op1=ALU.subtract), inc=DVE)
            S.emit("scalar", lambda e: e.activation(out=QS, in_=QS, func=AF.Ln,
                                                    bias=small[:, 6:7]),
                   waits=[(DVE, S.v[DVE])], inc=ACT)
            S.emit("scalar", lambda e: e.activation(out=QS, in_=QS, func=AF.Exp,
                                                    scale=-0.5), inc=ACT)
            for i in range(NT):
                xi = xtile(i)
                xn = XNORM[i % 2]
                S.emit("vector", (lambda i=i, xi=xi, xn=xn:
                                  lambda e: e.tensor_scalar(
                    out=xn, in0=xi, scalar1=MS[:, i:i + 1],
                    scalar2=QS[:, i:i + 1], op0=ALU.subtract, op1=ALU.mult))(),
                    waits=[(ACT, S.v[ACT])], inc=DVE)
                dvv = S.v[DVE]
                for k in range(KC):
                    S.emit("tensor", (lambda k=k, xn=xn: lambda e: e.transpose(
                        out=PBF[:, k * 128:(k + 1) * 128],
                        in_=xn[:, k * 128:(k + 1) * 128], identity=idm[:]))(),
                        waits=[(DVE, dvv), (ACT, S.v[ACT])], inc=PE)
                if evac_to_xt:
                    dst = XT[:].rearrange("p (k t) -> p k t", k=KC)[
                        :, :, i * P:(i + 1) * P]
                    S.emit("scalar", (lambda dst=dst: lambda e: e.copy(
                        out=dst, in_=PBF[:].rearrange("p (k t) -> p k t",
                                                      k=KC)))(),
                        waits=[(PE, S.v[PE])], inc=ACT)
                else:
                    S.emit("scalar", lambda e: e.copy(out=PACK, in_=PBF[:]),
                           waits=[(PE, S.v[PE]), (GD, S.v[GD])], inc=ACT)
                    for k in range(KC):
                        S.emit("gpsimd", (lambda k=k, i=i: lambda e: e.dma_start(
                            out=xT2_scr[k, :, i * P:(i + 1) * P],
                            in_=PACK[:, k * 128:(k + 1) * 128]))(),
                            waits=[(ACT, S.v[ACT])], inc=GD, inc_n=16)

        def emit_ffn(call):
            xt_done = S.v[ACT]
            S.emit("sync", lambda e: e.dma_start(out=SQ[0:E, :], in_=b_ex_d),
                   waits=[(DVE, S.v[DVE]), (ACT, S.v[ACT])], inc=LD, inc_n=16)
            S.emit("vector", lambda e: e.tensor_copy(b_ex_bf, SQ[0:E, :]),
                   waits=[(LD, S.v[LD])], inc=DVE)
            bexv = S.v[DVE]
            ps4_ready = 0
            for i in range(NT):
                for k in range(KC):
                    S.emit("tensor", (lambda i=i, k=k: lambda e: e.matmul(
                        out=PS[4][:, 0:E],
                        lhsT=XT[:, k * T + i * P: k * T + (i + 1) * P],
                        rhs=W_rtT_bf[:, k * E:(k + 1) * E],
                        start=(k == 0), stop=False))(),
                        waits=[(ACT, max(xt_done, ps4_ready))], inc=PE)
                S.emit("tensor", (lambda i=i: lambda e: e.matmul(
                    out=PS[4][:, 0:E], lhsT=ones_bf[:], rhs=b_rt_bf[:],
                    start=False, stop=True))(), inc=PE)
                S.emit("scalar", lambda e: e.copy(out=RL, in_=PS[4][:, 0:E]),
                       waits=[(PE, S.v[PE])], inc=ACT)
                ps4_ready = S.v[ACT]
                c0, c1 = small[:, 4:5], small[:, 5:6]
                S.emit("vector", lambda e: e.reduce_max(c0, RL, axis=AX.X),
                       waits=[(ACT, S.v[ACT])], inc=DVE)
                S.emit("vector", lambda e: e.tensor_scalar_mul(c0, c0, -1.0), inc=DVE)
                S.emit("scalar", lambda e: e.activation(out=RW[0], in_=RL,
                                                        func=AF.Exp, bias=c0),
                       waits=[(DVE, S.v[DVE])], inc=ACT)
                S.emit("vector", lambda e: e.reduce_sum(c1, RW[0], axis=AX.X),
                       waits=[(ACT, S.v[ACT])], inc=DVE)
                S.emit("vector", lambda e: e.reciprocal(c1, c1), inc=DVE)
                S.emit("vector", lambda e: e.tensor_scalar_mul(RW[1], RW[0], c1),
                       inc=DVE)
                S.emit("vector", lambda e: e.reduce_max(c0, RW[1], axis=AX.X), inc=DVE)
                S.emit("vector", lambda e: e.tensor_single_scalar(
                    RW[2], RW[1], c0, ALU.is_equal), inc=DVE)
                S.emit("vector", lambda e: e.scalar_tensor_tensor(
                    out=RW[2], in0=RW[2], scalar=-1e30, in1=RW[1], op0=ALU.mult,
                    op1=ALU.add), inc=DVE)
                S.emit("vector", lambda e: e.reduce_max(c0, RW[2], axis=AX.X), inc=DVE)
                S.emit("vector", lambda e: e.tensor_single_scalar(
                    RW[3], RW[1], c0, ALU.is_ge), inc=DVE)
                S.emit("vector", (lambda i=i: lambda e: e.scalar_tensor_tensor(
                    out=w_all[:, i * E:(i + 1) * E], in0=RW[1], scalar=0.5,
                    in1=RW[3], op0=ALU.mult, op1=ALU.mult))(), inc=DVE)
            for i in range(NT):
                S.emit("vector", (lambda i=i: lambda e: e.tensor_copy(
                    vbf(AR1, 13872, E), w_all[:, i * E:(i + 1) * E]))(),
                    waits=[(ACT, S.v[ACT])], inc=DVE)
                S.emit("tensor", (lambda i=i: lambda e: e.transpose(
                    out=PBF[0:E, 0:P], in_=vbf(AR1, 13872, E),
                    identity=idm[:]))(), waits=[(DVE, S.v[DVE])], inc=PE)
                S.emit("scalar", (lambda i=i: lambda e: e.copy(
                    out=wT8_bf[:, i * P:(i + 1) * P], in_=PBF[0:E, 0:P]))(),
                    waits=[(PE, S.v[PE])], inc=ACT)
            bias_ready = bexv
            wT8_done = S.v[ACT]
            for i in range(NT):
                for nt in range(2):
                    S.emit("tensor", (lambda i=i, nt=nt: lambda e: e.matmul(
                        out=PS[4 + nt][:], lhsT=wT8_bf[:, i * P:(i + 1) * P],
                        rhs=b_ex_bf[:, nt * 512:(nt + 1) * 512],
                        start=True, stop=True))(),
                        waits=[(ACT, wT8_done), (DVE, bias_ready)], inc=PE)
                pev = S.v[PE]
                for nt in range(2):
                    S.emit("vector", (lambda i=i, nt=nt: lambda e: e.tensor_add(
                        xtile(i)[:, nt * 512:(nt + 1) * 512], PS[4 + nt][:],
                        xtile(i)[:, nt * 512:(nt + 1) * 512]))(),
                        waits=[(PE, pev)], inc=DVE)
                bias_ready = S.v[DVE]
            bank_ready = [0, 0, 0, 0]
            wt_free = [0, 0]
            stg_free = [S.v[DVE]] * 4
            stgb_free = [0, 0]
            for e_i in range(E):
                slot = e_i % 2
                if call == 0:
                    for ot in range(KC):
                        stg = WSTG[ot % 4]
                        stgb = WSTG_BF[ot % 2]
                        qeng = "sync" if ot % 2 == 0 else "gpsimd"
                        wsem = "s_ws%d" % (ot % 4)
                        S.emit(qeng, (lambda e_i=e_i, ot=ot, stg=stg:
                                      lambda e: e.dma_start(
                            out=stg, in_=W_ex_d[e_i, ot * 128:(ot + 1) * 128, :]))(),
                            waits=[(DVE, stg_free[ot % 4])], inc=wsem, inc_n=16)
                        S.emit("vector", (lambda stg=stg, stgb=stgb:
                                          lambda e: e.tensor_copy(stgb, stg))(),
                            waits=[(wsem, S.v[wsem]), (PE, stgb_free[ot % 2])],
                            inc=DVE)
                        stg_free[ot % 4] = S.v[DVE]
                        dvv = S.v[DVE]
                        for k in range(KC):
                            S.emit("tensor", (lambda k=k, stgb=stgb:
                                              lambda e: e.transpose(
                                out=PBF[:, k * 128:(k + 1) * 128],
                                in_=stgb[:, k * 128:(k + 1) * 128],
                                identity=idm[:]))(),
                                waits=[(DVE, dvv), (ACT, S.v[ACT])], inc=PE)
                        stgb_free[ot % 2] = S.v[PE]
                        dst = WTv(slot).rearrange("p (k o) -> p k o", k=KC)[
                            :, :, ot * 128:(ot + 1) * 128]
                        S.emit("scalar", (lambda dst=dst: lambda e: e.copy(
                            out=dst,
                            in_=PBF[:].rearrange("p (k b) -> p k b", k=KC)))(),
                            waits=[(PE, S.v[PE])], inc=ACT)
                    S.emit("gpsimd", (lambda e_i=e_i, slot=slot:
                                      lambda e: e.dma_start(
                        out=wT_scr[e_i], in_=WTv(slot)))(),
                        waits=[(ACT, S.v[ACT])], inc=GD, inc_n=16)
                    wt_ok = [(ACT, S.v[ACT])]
                else:
                    wsem = "s_ws%d" % slot
                    S.emit("sync", (lambda e_i=e_i, slot=slot:
                                    lambda e: e.dma_start(
                        out=WTv(slot), in_=wT_scr[e_i]))(),
                        waits=[(PE, wt_free[slot])], inc=wsem, inc_n=16)
                    wt_ok = [(wsem, S.v[wsem])]
                for i in range(NT):
                    for nt in range(2):
                        qi = slot * 2 + nt
                        q = PS[qi]
                        for k in range(KC):
                            S.emit("tensor", (lambda i=i, nt=nt, k=k, q=q, slot=slot:
                                              lambda e: e.matmul(
                                out=q[:],
                                lhsT=XT[:, k * T + i * P: k * T + (i + 1) * P],
                                rhs=WTv(slot).rearrange("p (k o) -> p k o", k=KC)[
                                    :, k, nt * 512:(nt + 1) * 512],
                                start=(k == 0), stop=(k == KC - 1)))(),
                                waits=wt_ok + [(DVE, bank_ready[qi])], inc=PE)
                        pev = S.v[PE]
                        S.emit("vector", (lambda i=i, nt=nt, q=q, e_i=e_i:
                                          lambda e: e.scalar_tensor_tensor(
                            out=xtile(i)[:, nt * 512:(nt + 1) * 512], in0=q[:],
                            scalar=w_all[:, i * E + e_i:i * E + e_i + 1],
                            in1=xtile(i)[:, nt * 512:(nt + 1) * 512],
                            op0=ALU.mult, op1=ALU.add))(),
                            waits=[(PE, pev)], inc=DVE)
                        bank_ready[qi] = S.v[DVE]
                wt_free[slot] = S.v[PE]

        emit_ln_all(evac_to_xt=True)
        if upto == "ln1":
            S.emit("vector", lambda e: e.tensor_copy(v32(AR1, 0, 2048),
                                                     XT[:, 0:2048]),
                   waits=[(ACT, S.v[ACT])], inc=DVE)
            S.emit("gpsimd", lambda e: e.dma_start(
                out=x_scr[0:2].rearrange("a p d -> p a d"),
                in_=v32(AR1, 0, 2048)), waits=[(DVE, S.v[DVE])],
                inc=GD, inc_n=16)
            for i in range(NT):
                S.emit("gpsimd", (lambda i=i: lambda e: e.dma_start(
                    out=out_d[i * P:(i + 1) * P, :], in_=xtile(i)))(),
                    waits=[(DVE, S.v[DVE]), (ACT, S.v[ACT])], inc=GD, inc_n=16)
            S.emit("gpsimd", lambda e: e.nop(), waits=[(GD, S.v[GD])])
            S.finalize(block)
            return nc
        emit_ffn(0)
        emit_ln_all(evac_to_xt=False)
        for i in range(NT):
            S.emit("gpsimd", (lambda i=i: lambda e: e.dma_start(
                out=x_scr[i], in_=xtile(i)))(),
                waits=[(DVE, S.v[DVE])], inc=GD, inc_n=16)

        if upto == "ffn1":
            for i in range(NT):
                S.emit("gpsimd", (lambda i=i: lambda e: e.dma_start(
                    out=out_d[i * P:(i + 1) * P, :], in_=xtile(i)))(),
                    waits=[(DVE, S.v[DVE]), (ACT, S.v[ACT])], inc=GD, inc_n=16)
            if debug_scr:
                S.emit("gpsimd", lambda e: e.dma_start(
                    out=x_scr[0, :, 0:NT * E], in_=w_all[:, :]),
                    waits=[(DVE, S.v[DVE])], inc=GD, inc_n=16)
                S.emit("vector", lambda e: e.tensor_copy(SQ, XT[:, 0:1024]),
                       inc=DVE)
                S.emit("gpsimd", lambda e: e.dma_start(
                    out=x_scr[1], in_=SQ), waits=[(DVE, S.v[DVE])],
                    inc=GD, inc_n=16)
            S.emit("gpsimd", lambda e: e.nop(), waits=[(GD, S.v[GD])])
            S.finalize(block)
            return nc

        brep_free = 0
        ugd_last = 0
        dtu_dma_prev = 0
        ld_cnt, dta_cnt, dbx_cnt, ydma_gd = {}, {}, {}, {}
        for g2 in range(G):
            S.emit("vector", (lambda g2=g2: lambda e: e.memset(
                xinv(g2, 4, off=0), 0.0))(),
                waits=[(GD, S.v[GD]), (ACT, S.v[ACT])], inc=DVE)
        for c in range(NCH):
            for h in range(H):
                S.emit("sync", (lambda c=c, h=h: lambda e: e.dma_start(
                    out=XT2C[c % 2].rearrange("p (h t) -> p h t", h=H)[:, h, :],
                    in_=xT2_scr[h, :, c * C:(c + 1) * C]))(),
                    waits=[(PE, S.v[PE]), (GD, S.v[GD])], inc=LD, inc_n=16)
            for s_ in range(2):
                S.emit("gpsimd", (lambda c=c, s_=s_: lambda e: e.dma_start(
                    out=XRES[s_], in_=x_scr[2 * c + s_]))(),
                    waits=[(DVE, S.v[DVE])], inc="s_xr", inc_n=16)
            xres_gd = S.v["s_xr"]
            ld_xt2 = S.v[LD]
            for h in range(H):
                pz_a, pz_b = PS[(h % 2) * 2], PS[(h % 2) * 2 + 1]
                rhs = XT2C[c % 2].rearrange("p (hh t) -> p hh t", hh=H)[:, h, :]
                for mi in range(4):
                    q = pz_a if mi < 2 else pz_b
                    col = (mi % 2) * 256
                    S.emit("tensor", (lambda h=h, mi=mi, q=q, col=col, rhs=rhs:
                                      lambda e: e.matmul(
                        out=q[:, col:col + 256],
                        lhsT=W_in_bf[:, h * 512 + mi * 128: h * 512 + (mi + 1) * 128],
                        rhs=rhs, start=True, stop=True))(),
                        waits=[(LD, ld_xt2), (ACT, S.v[ACT]), (DVE, S.v[DVE])],
                        inc=PE)
                pev = S.v[PE]
                g2 = 2 * h
                xin_pair = AR2[:, g2 * (C + 4):(g2 + 2) * (C + 4)].rearrange(
                    "p (two cc) -> p two cc", two=2)[:, :, 4:4 + C]
                S.emit("scalar", (lambda xin_pair=xin_pair, pz_a=pz_a:
                                  lambda e: e.copy(
                    out=xin_pair,
                    in_=pz_a[:].rearrange("p (two cc) -> p two cc", two=2)))(),
                    waits=[(PE, pev), (DVE, S.v[DVE])], inc=ACT)
                S.emit("scalar", (lambda g2=g2, pz_b=pz_b: lambda e: e.activation(
                    out=v32(AR2, SZOFF + g2 * C, 2 * C), in_=pz_b[:],
                    func=AF.Silu))(), inc=ACT)
                actv = S.v[ACT]
                for dh in range(2):
                    g2 = 2 * h + dh
                    acc = CONVACC[dh]
                    S.emit("vector", (lambda g2=g2, acc=acc: lambda e: e.tensor_scalar(
                        out=acc, in0=xinv(g2, C, off=1),
                        scalar1=conv_w_s[:, g2 * 4:g2 * 4 + 1],
                        scalar2=conv_b_s[:, g2:g2 + 1],
                        op0=ALU.mult, op1=ALU.add))(),
                        waits=[(ACT, actv)], inc=DVE)
                    for j in range(1, 4):
                        S.emit("vector", (lambda g2=g2, j=j, acc=acc:
                                          lambda e: e.scalar_tensor_tensor(
                            out=acc, in0=xinv(g2, C, off=1 + j),
                            scalar=conv_w_s[:, g2 * 4 + j:g2 * 4 + j + 1],
                            in1=acc, op0=ALU.mult, op1=ALU.add))(), inc=DVE)
                    S.emit("vector", (lambda g2=g2: lambda e: e.tensor_copy(
                        xinv(g2, 4, off=0), xinv(g2, 4, off=C)))(), inc=DVE)
                dvv = S.v[DVE]
                g2p = 2 * h
                xin_pair2 = AR2[:, g2p * (C + 4):(g2p + 2) * (C + 4)].rearrange(
                    "p (two cc) -> p two cc", two=2)[:, :, 4:4 + C]
                S.emit("scalar", (lambda xin_pair2=xin_pair2:
                                  lambda e: e.activation(
                    out=xin_pair2,
                    in_=v32(AR2, 15168, 2 * C).rearrange(
                        "p (two cc) -> p two cc", two=2),
                    func=AF.Silu))(), waits=[(DVE, dvv)], inc=ACT)
                S.emit("vector", (lambda g2p=g2p, xin_pair2=xin_pair2:
                                  lambda e: e.tensor_copy(
                    AR3[:].bitcast(BF16)[:, g2p * 256:(g2p + 2) * 256].rearrange(
                        "p (two cc) -> p two cc", two=2), xin_pair2))(),
                    waits=[(ACT, S.v[ACT])], inc=DVE)
            for h in range(H):
                xc0, xc1 = xcbfv(2 * h), xcbfv(2 * h + 1)
                mm = [(PS[4], 0, 0, DTR), (PS[4], 256, DTR, DTR + NS),
                      (PS[5], 0, DTR + NS, DTR + 2 * NS)]
                for (q, col, m0, m1) in mm:
                    for k in range(2):
                        rhs = xc0 if k == 0 else xc1
                        S.emit("tensor", (lambda q=q, col=col, m0=m0, m1=m1, k=k,
                                          rhs=rhs, h=h: lambda e: e.matmul(
                            out=q[0:m1 - m0, col:col + 256],
                            lhsT=W_xp_bf[:, h * 80 + k * 40 + m0:
                                         h * 80 + k * 40 + m1],
                            rhs=rhs, start=(k == 0), stop=(k == 1)))(),
                            waits=[(DVE, S.v[DVE]), (ACT, S.v[ACT])], inc=PE)
                pev = S.v[PE]
                S.emit("scalar", lambda e: e.copy(out=dtr_bf[:],
                                                  in_=PS[4][0:DTR, 0:256]),
                       waits=[(PE, pev), (GD, S.v[GD])], inc=ACT)
                S.emit("scalar", lambda e: e.copy(out=bcs[0:NS, 0:C],
                                                  in_=PS[4][0:NS, 256:512]),
                       inc=ACT)
                S.emit("scalar", lambda e: e.copy(out=bcs[0:NS, C:2 * C],
                                                  in_=PS[5][0:NS, 0:256]), inc=ACT)
                actv = S.v[ACT]
                bsem = "s_bch%d" % h
                S.emit("gpsimd", (lambda h=h, c=c: lambda e: e.dma_start(
                    out=bc_scr[h, 0, :, c * C:(c + 1) * C], in_=bcs[0:NS, 0:C]))(),
                    waits=[(ACT, actv)], inc=bsem, inc_n=16)
                S.emit("gpsimd", (lambda h=h, c=c: lambda e: e.dma_start(
                    out=bc_scr[h, 1, :, c * C:(c + 1) * C],
                    in_=bcs[0:NS, C:2 * C]))(), waits=[(ACT, actv)],
                    inc=bsem, inc_n=16)
                brep3 = BREP.rearrange("p (n t) -> p n t", n=NS)
                crep3 = CREP.rearrange("p (n t) -> p n t", n=NS)
                srcB = bc_scr[h, 0, :, c * C:(c + 1) * C].unsqueeze(0)\
                    .broadcast_to([NS, NS, C])
                srcC = bc_scr[h, 1, :, c * C:(c + 1) * C].unsqueeze(0)\
                    .broadcast_to([NS, NS, C])
                S.emit("sync", (lambda h=h, srcB=srcB, brep3=brep3:
                                lambda e: e.dma_start(
                    out=brep3[h * NS:(h + 1) * NS], in_=srcB))(),
                    waits=[(bsem, S.v[bsem]), (DVE, brep_free)],
                    inc="s_br", inc_n=16)
                S.emit("sync", (lambda h=h, srcC=srcC, crep3=crep3:
                                lambda e: e.dma_start(
                    out=crep3[h * NS:(h + 1) * NS], in_=srcC))(),
                    waits=[(bsem, S.v[bsem]), (DVE, brep_free)],
                    inc="s_br", inc_n=16)
                for m in range(2):
                    S.emit("tensor", (lambda h=h, m=m: lambda e: e.matmul(
                        out=PS[3][:, m * 256:(m + 1) * 256],
                        lhsT=W_dt_bf[:, h * DI + m * 128: h * DI + (m + 1) * 128],
                        rhs=dtr_bf[:], start=True, stop=True))(),
                        waits=[(ACT, actv), (DVE, S.v[DVE])], inc=PE)
                pev = S.v[PE]
                g2 = 2 * h
                X2 = v32(AR2, 14400, 2 * C)
                E2 = v32(AR2, 15168, 2 * C)
                dt_pair = v32(AR2, DTOFF + g2 * C, 2 * C)
                S.emit("scalar", lambda e: e.copy(out=X2, in_=PS[3][:]),
                       waits=[(PE, pev), (DVE, S.v[DVE])], inc=ACT)
                for dh in range(2):
                    S.emit("vector", (lambda g2=g2, dh=dh: lambda e:
                                      e.tensor_scalar_add(
                        v32(AR2, 14400 + dh * C, C), v32(AR2, 14400 + dh * C, C),
                        b_dt_s[:, g2 + dh:g2 + dh + 1]))(),
                        waits=[(ACT, S.v[ACT])], inc=DVE)
                S.emit("vector", lambda e: e.scalar_tensor_tensor(
                    out=E2, in0=X2, scalar=-1.0, in1=X2,
                    op0=ALU.mult, op1=ALU.max), inc=DVE)
                S.emit("scalar", lambda e: e.activation(
                    out=E2, in_=E2, func=AF.Exp, scale=-1.0),
                    waits=[(DVE, S.v[DVE])], inc=ACT)
                S.emit("scalar", lambda e: e.activation(
                    out=E2, in_=E2, func=AF.Ln, bias=small[:, 7:8]), inc=ACT)
                S.emit("vector", lambda e: e.tensor_scalar_max(X2, X2, 0.0),
                       waits=[(ACT, S.v[ACT])], inc=DVE)
                S.emit("vector", (lambda dt_pair=dt_pair, X2=X2, E2=E2:
                                  lambda e: e.tensor_add(dt_pair, X2, E2))(),
                       waits=[(GD, dtu_dma_prev)], inc=DVE)
                xc_pair = AR2[:, g2 * (C + 4):(g2 + 2) * (C + 4)].rearrange(
                    "p (two cc) -> p two cc", two=2)[:, :, 4:4 + C]
                S.emit("vector", (lambda dt_pair=dt_pair, xc_pair=xc_pair:
                                  lambda e: e.tensor_mul(
                    v32(AR2, 15680, 2 * C).rearrange("p (two cc) -> p two cc",
                                                     two=2),
                    dt_pair.rearrange("p (two cc) -> p two cc", two=2),
                    xc_pair))(), waits=[(GD, ugd_last)], inc=DVE)
                dvv = S.v[DVE]
                for dh in range(2):
                    g2b = 2 * h + dh
                    S.emit("gpsimd", (lambda g2b=g2b, h=h, dh=dh, c=c:
                                      lambda e: e.dma_start(
                        out=dt_scr[h, dh * 128:(dh + 1) * 128, c * C:(c + 1) * C],
                        in_=dtv(g2b)))(), waits=[(DVE, dvv)], inc=GD, inc_n=16)
                    S.emit("gpsimd", (lambda h=h, dh=dh, c=c:
                                      lambda e: e.dma_start(
                        out=u_scr[h, dh * 128:(dh + 1) * 128, c * C:(c + 1) * C],
                        in_=UBLK[dh]))(), waits=[(DVE, dvv)], inc=GD, inc_n=16)
                ugd_last = S.v[GD]
            dtu_dma_prev = S.v[GD]
            ld_bc = S.v["s_br"]
            gd_dtu = S.v[GD]

            def emit_g_load(g):
                gg = c * G + g
                sl = g % 2
                sem = "s_dti0" if sl == 0 else "s_dti1"
                src_dt = dt_scr[:, g * NS:(g + 1) * NS, c * C:(c + 1) * C]
                src_u = u_scr[:, g * NS:(g + 1) * NS, c * C:(c + 1) * C]
                S.emit("sync", (lambda src_dt=src_dt, sl=sl: lambda e: e.dma_start(
                    out=DTI[sl], in_=src_dt))(),
                    waits=[(GD, gd_dtu), (DVE, dta_cnt.get(gg - 2, 0))],
                    inc=sem, inc_n=16)
                S.emit("sync", (lambda src_u=src_u, sl=sl: lambda e: e.dma_start(
                    out=UI[sl], in_=src_u))(),
                    waits=[(DVE, dbx_cnt.get(gg - 2, 0))], inc=sem, inc_n=16)
                ld_cnt[gg] = S.v[sem]

            emit_g_load(0)
            for g in range(G):
                sl = g % 2
                gg = c * G + g
                ldsem = "s_dti0" if sl == 0 else "s_dti1"
                ydsem = "s_yd0" if sl == 0 else "s_yd1"
                da3 = DA[sl].rearrange("p (n t) -> p n t", n=NS)
                S.emit("vector", (lambda g=g, sl=sl, da3=da3: lambda e: e.tensor_tensor(
                    out=da3,
                    in0=DTI[sl].unsqueeze(1).broadcast_to([P, NS, C]),
                    in1=A_neg[:, g * NS:(g + 1) * NS].unsqueeze(2)
                    .broadcast_to([P, NS, C]), op=ALU.mult))(),
                    waits=[(ldsem, ld_cnt[gg]), (ACT, S.v[ACT]),
                           (ydsem, ydma_gd.get(gg - 2, 0))], inc=DVE)
                dta_cnt[gg] = S.v[DVE]
                S.emit("scalar", (lambda sl=sl: lambda e: e.activation(
                    out=DA[sl], in_=DA[sl], func=AF.Exp))(),
                    waits=[(DVE, S.v[DVE])], inc=ACT)
                dbx3 = DBX[sl].rearrange("p (n t) -> p n t", n=NS)
                S.emit("vector", (lambda sl=sl, dbx3=dbx3: lambda e: e.tensor_tensor(
                    out=dbx3,
                    in0=UI[sl].unsqueeze(1).broadcast_to([P, NS, C]),
                    in1=BREP.rearrange("p (n t) -> p n t", n=NS),
                    op=ALU.mult))(), waits=[("s_br", ld_bc)], inc=DVE)
                dbx_cnt[gg] = S.v[DVE]
                if g + 1 < G:
                    emit_g_load(g + 1)
                da0 = DA[sl].rearrange("p (n t) -> p n t", n=NS)[:, :, 0]
                db0 = DBX[sl].rearrange("p (n t) -> p n t", n=NS)[:, :, 0]
                tmpv = small[:, 16:32]
                S.emit("vector", (lambda g=g, da0=da0, tmpv=tmpv:
                                  lambda e: e.tensor_mul(
                    tmpv, da0, carry[:, g * NS:(g + 1) * NS]))(),
                    waits=[(ACT, S.v[ACT])], inc=DVE)
                S.emit("vector", (lambda da0=da0: lambda e: e.memset(da0, 0.0))(),
                       inc=DVE)
                S.emit("vector", (lambda db0=db0, tmpv=tmpv:
                                  lambda e: e.tensor_add(db0, tmpv, db0))(), inc=DVE)
                S.emit("vector", (lambda sl=sl: lambda e: e.tensor_tensor_scan(
                    out=DBX[sl], data0=DA[sl], data1=DBX[sl], initial=0.0,
                    op0=ALU.mult, op1=ALU.add))(), inc=DVE)
                hv3 = DBX[sl].rearrange("p (n t) -> p n t", n=NS)
                S.emit("vector", (lambda g=g, hv3=hv3: lambda e: e.tensor_copy(
                    carry[:, g * NS:(g + 1) * NS], hv3[:, :, C - 1]))(), inc=DVE)
                S.emit("vector", (lambda sl=sl: lambda e: e.tensor_tensor(
                    out=DA[sl], in0=DBX[sl], in1=CREP, op=ALU.mult))(), inc=DVE)
                width = NS
                while width > 1:
                    half = width // 2
                    a = DA[sl][:, 0:half * C]
                    b = DA[sl][:, half * C:width * C]
                    S.emit("vector", (lambda a=a, b=b: lambda e: e.tensor_add(
                        a, a, b))(), inc=DVE)
                    width = half
                S.emit("vector", (lambda sl=sl: lambda e: e.tensor_copy(
                    YI[sl], DA[sl][:, 0:C]))(),
                    waits=[(ydsem, S.v[ydsem])], inc=DVE)
                dvv = S.v[DVE]
                dst_y = y_scr[:, g * NS:(g + 1) * NS, c * C:(c + 1) * C]
                S.emit("gpsimd", (lambda dst_y=dst_y, sl=sl: lambda e: e.dma_start(
                    out=dst_y, in_=YI[sl]))(), waits=[(DVE, dvv)],
                    inc=ydsem, inc_n=16)
                ydma_gd[gg] = S.v[ydsem]
            yd0_all, yd1_all = S.v["s_yd0"], S.v["s_yd1"]
            brep_free = S.v[DVE]
            yld = {}
            for g2 in range(G):
                h, dh = g2 // 2, g2 % 2
                S.emit("sync", (lambda h=h, dh=dh, c=c, g2=g2:
                                lambda e: e.dma_start(
                    out=yblk(g2),
                    in_=y_scr[h, dh * 128:(dh + 1) * 128,
                              c * C:(c + 1) * C]))(),
                    waits=[("s_yd0", yd0_all), ("s_yd1", yd1_all),
                           (DVE, S.v[DVE])], inc="s_yl", inc_n=16)
            yld_all = S.v["s_yl"]
            for h in range(H):
                for dh in range(2):
                    g2 = 2 * h + dh
                    S.emit("vector", (lambda g2=g2: lambda e: e.scalar_tensor_tensor(
                        out=yblk(g2), in0=xinv(g2, C, off=4),
                        scalar=Dp_s[:, g2:g2 + 1], in1=yblk(g2),
                        op0=ALU.mult, op1=ALU.add))(),
                        waits=[("s_yl", yld_all)], inc=DVE)
                g2 = 2 * h
                S.emit("vector", (lambda g2=g2: lambda e: e.tensor_mul(
                    AR3[:].bitcast(BF16)[:, 4096 + g2 * 256:4096 + (g2 + 2) * 256],
                    v32(AR2, DTOFF + g2 * C, 2 * C),
                    v32(AR2, SZOFF + g2 * C, 2 * C)))(), inc=DVE)
                dvv = S.v[DVE]
                for tsub in range(2):
                    for k in range(2):
                        S.emit("tensor", (lambda h=h, tsub=tsub, k=k:
                                          lambda e: e.matmul(
                            out=PS[5][:, tsub * 128:(tsub + 1) * 128],
                            lhsT=ybfv(2 * h + k)[:, tsub * 128:(tsub + 1) * 128],
                            rhs=W_out_bf[:, (h * 2 + k) * DH:(h * 2 + k + 1) * DH],
                            start=(k == 0), stop=(k == 1)))(),
                            waits=[(DVE, dvv)], inc=PE)
                pev = S.v[PE]
                for tsub in range(2):
                    S.emit("vector", (lambda h=h, tsub=tsub: lambda e: e.tensor_add(
                        XRES[tsub][:, h * 128:(h + 1) * 128],
                        PS[5][:, tsub * 128:(tsub + 1) * 128],
                        XRES[tsub][:, h * 128:(h + 1) * 128]))(),
                        waits=[(PE, pev), ("s_xr", xres_gd)], inc=DVE)
            for s_ in range(2):
                S.emit("gpsimd", (lambda c=c, s_=s_: lambda e: e.dma_start(
                    out=x_scr[2 * c + s_], in_=XRES[s_]))(),
                    waits=[(DVE, S.v[DVE])], inc=GD, inc_n=16)

        gd_m = S.v[GD]
        for i in range(NT):
            S.emit("sync", (lambda i=i: lambda e: e.dma_start(
                out=xtile(i), in_=x_scr[i]))(),
                waits=[(GD, gd_m), (DVE, S.v[DVE]), (ACT, S.v[ACT]),
                       (PE, S.v[PE])], inc=LD, inc_n=16)
        if upto != "mamba":
            emit_ln_all(evac_to_xt=True)
            emit_ffn(1)
        for i in range(NT):
            S.emit("gpsimd", (lambda i=i: lambda e: e.dma_start(
                out=out_d[i * P:(i + 1) * P, :], in_=xtile(i)))(),
                waits=[(DVE, S.v[DVE]), (ACT, S.v[ACT])], inc=GD, inc_n=16)
        S.emit("gpsimd", lambda e: e.nop(), waits=[(GD, S.v[GD])])

        S.finalize(block)
    return nc


def kernel(L0_attn_x, L0_attn_W_in, L0_attn_conv_w, L0_attn_conv_b, L0_attn_W_xp,
           L0_attn_W_dt, L0_attn_b_dt, L0_attn_A_log, L0_attn_Dp, L0_attn_W_out,
           L0_ffn_W_experts, L0_ffn_b_experts, L0_ffn_W_router, L0_ffn_b_router):
    nc = build()
    idm = np.eye(P, dtype=ml_dtypes.bfloat16)
    base = dict(
        W_in=np.ascontiguousarray(L0_attn_W_in, np.float32),
        conv_w=np.ascontiguousarray(L0_attn_conv_w, np.float32),
        conv_b=np.ascontiguousarray(L0_attn_conv_b, np.float32),
        W_xp=np.ascontiguousarray(L0_attn_W_xp, np.float32),
        W_dt=np.ascontiguousarray(L0_attn_W_dt, np.float32),
        b_dt=np.ascontiguousarray(L0_attn_b_dt, np.float32),
        A_log=np.ascontiguousarray(L0_attn_A_log, np.float32),
        Dp=np.ascontiguousarray(L0_attn_Dp, np.float32),
        W_out=np.ascontiguousarray(L0_attn_W_out, np.float32),
        W_ex=np.ascontiguousarray(L0_ffn_W_experts, np.float32),
        b_ex=np.ascontiguousarray(L0_ffn_b_experts, np.float32),
        W_rt=np.ascontiguousarray(L0_ffn_W_router, np.float32),
        b_rt=np.ascontiguousarray(L0_ffn_b_router, np.float32).reshape(1, E),
        idm=idm,
    )
    x = np.ascontiguousarray(L0_attn_x, np.float32)
    in_maps = [dict(x=x[c], **base) for c in range(8)]
    res = run_bass_kernel_spmd(nc, in_maps, core_ids=list(range(8)))
    return np.stack([np.asarray(res.results[c]["out"]) for c in range(8)])
